# version 05
# baseline (speedup 1.0000x reference)
import numpy as np
import ml_dtypes

import concourse.bass as bass
import concourse.mybir as mybir
from concourse.bass_utils import run_bass_kernel_spmd

F32 = mybir.dt.float32
BF16 = mybir.dt.bfloat16
AF = mybir.ActivationFunctionType
ALU = mybir.AluOpType
AX = mybir.AxisListType

NCORES = 8
S = 2048
BLOC = 2
T = BLOC * S
D = 1024
DT = D // 128
TC = 512
MAGIC = 3 * 2.0 ** 22
LN_EPS = 1e-5
BF = ml_dtypes.bfloat16



def to_cols(v):
    n = v.shape[0]
    return np.ascontiguousarray(
        np.asarray(v, np.float32).reshape(n // 128, 128).T)


def to_T(a):
    t, f = a.shape
    ft = f // 128
    return np.ascontiguousarray(
        np.asarray(a).T.reshape(ft, 128, t).transpose(1, 0, 2)
        .reshape(128, ft * t))


def from_T(a, tokens, feat):
    ft = feat // 128
    return np.ascontiguousarray(
        a.reshape(128, ft, tokens).transpose(1, 0, 2)
        .reshape(feat, tokens).T)



class Sem:
    __slots__ = ("h", "n", "name")

    def __init__(self, h, name):
        self.h = h
        self.n = 0
        self.name = name


class Prog:
    ENGINES = ("sync", "gpsimd", "tensor", "scalar", "vector")

    def __init__(self, nc):
        self.nc = nc
        self.q = {e: [] for e in self.ENGINES}
        self._cms = []

    def sem(self, name):
        cm = self.nc.semaphore(name)
        h = cm.__enter__()
        self._cms.append(cm)
        return Sem(h, name)

    def step(self, eng, fn, waits=(), inc=None, inc_k=1):
        wl = [(s.h, v) for s, v in waits if v > 0]
        if inc is not None:
            inc.n += inc_k
            tgt = (inc.h, inc_k)
        else:
            tgt = None

        def run(E, wl=wl, fn=fn, tgt=tgt):
            for h, v in wl:
                E.wait_ge(h, v)
            inst = fn(E)
            if tgt is not None:
                inst.then_inc(tgt[0], tgt[1])
        self.q[eng].append(run)

    def wait(self, eng, waits):
        wl = [(s.h, v) for s, v in waits if v > 0]
        if wl:
            self.q[eng].append(
                lambda E, wl=wl: [E.wait_ge(h, v) for h, v in wl])

    def emit(self):
        with self.nc.Block() as block:
            for e in self.ENGINES:
                if self.q[e]:
                    def body(E, e=e):
                        for f in self.q[e]:
                            f(E)
                    getattr(block, e)(body)
        for cm in reversed(self._cms):
            cm.__exit__(None, None, None)


class Ctx:
    def __init__(self):
        nc = bass.Bass("TRN2", target_bir_lowering=False)
        self.nc = nc
        self.p = Prog(nc)
        p = self.p
        self.s_x = p.sem("s_x")
        self.s_w = p.sem("s_w")
        self.s_wf = p.sem("s_wf")
        self.s_wl = [p.sem("s_wl0"), p.sem("s_wl1"),
                     p.sem("s_wl2"), p.sem("s_wl3")]
        self.s_mm = p.sem("s_mm")
        self.s_sc = p.sem("s_sc")
        self.s_ve = p.sem("s_ve")
        self.s_st = p.sem("s_st")
        self.s_cc = p.sem("s_cc")
        self.s_j = p.sem("s_j")
        self.B = [nc.alloc_psum_tensor(f"pb{i}", [128, 512], F32)
                  for i in range(5)]
        self.U = [nc.alloc_psum_tensor(f"pu{i}", [128, 512], F32)
                  for i in range(2)]
        self.TP = nc.alloc_psum_tensor("ptp", [128, 1024], BF16)
        self._bank_rel = [None] * 8
        self._aid = 0
        self.identity = nc.alloc_sbuf_tensor("sb_identity", [128, 128], BF16)
        self.epst = nc.alloc_sbuf_tensor("sb_eps", [128, 8], F32)
        self.eps_cols = {}
        self.ones_col = nc.alloc_sbuf_tensor("sb_ones_col", [128, 1], BF16)
        self.ones_row = nc.alloc_sbuf_tensor("sb_ones_row", [1, 128], F32)

    def setup_consts(self):
        nc = self.nc
        ident = nc.dram_tensor("c_identity", [128, 128], F32,
                               kind="ExternalInput")
        p = self.p
        p.step("gpsimd", lambda E: E.dma_start(out=self.identity[:, :],
                                               in_=ident.ap()),
               inc=self.s_w, inc_k=16)
        p.step("vector", lambda E: E.memset(self.ones_col[:, :], 1.0),
               inc=self.s_ve)
        p.step("vector", lambda E: E.memset(self.ones_row[:, :], 1.0),
               inc=self.s_ve)
        for v in (1e-5, 1.1920929e-07, 1e-6):
            self.eps_ap(v)
        self.const_ready = [(self.s_w, self.s_w.n), (self.s_ve, self.s_ve.n)]
        p.wait("tensor", self.const_ready)
        p.wait("scalar", self.const_ready)

    def eps_ap(self, v):
        v = float(v)
        if v not in self.eps_cols:
            i = len(self.eps_cols)
            self.eps_cols[v] = i
            self.p.step("vector", lambda E, i=i, v=v: E.memset(
                self.epst[:, i:i + 1], v), inc=self.s_ve)
        i = self.eps_cols[v]
        return self.epst[:, i:i + 1]

    def sb(self, name, shape, dtype):
        self._aid += 1
        return self.nc.alloc_sbuf_tensor(f"{name}_{self._aid}", shape, dtype)

    def scope(self):
        from contextlib import ExitStack
        return ExitStack()

    def sbs(self, es, name, shape, dtype):
        self._aid += 1
        return es.enter_context(
            self.nc.sbuf_tensor(f"{name}_{self._aid}", shape, dtype))

    def join(self):
        p = self.p
        for e in Prog.ENGINES:
            p.step(e, lambda E: E.nop(), inc=self.s_j)
        tgt = self.s_j.n
        dw = [(self.s_x, self.s_x.n), (self.s_w, self.s_w.n),
              (self.s_st, self.s_st.n)] + \
            [(s, s.n) for s in self.s_wl]
        for e in Prog.ENGINES:
            p.wait(e, [(self.s_j, tgt)] + dw)

    def bank_acquire(self, i):
        r = self._bank_rel[i]
        return [] if r is None else [r]

    def note_bank(self, i, sem):
        self._bank_rel[i] = (sem, sem.n)



def linear_layer(C, *, x_sb, KT, W, n_tiles, evac, evac_eng="scalar",
                 x_base=0, x_Ttot=T, tok0=0, Tlen=None, slab_nt=4,
                 wbufs=None, banks=(0, 1, 2, 3), tc_outer=False,
                 chunk_hook=None):
    p = C.p
    if Tlen is None:
        Tlen = x_Ttot
    n_tc = Tlen // TC
    if isinstance(n_tiles, int):
        n_tiles = [(i * 128, 128) for i in range(n_tiles)]
    NTT = len(n_tiles)
    slabs = (NTT + slab_nt - 1) // slab_nt
    if tc_outer:
        assert slabs == 1
    if wbufs is None:
        nb = 1 if slabs == 1 else 2
        totw = sum(w for _, w in n_tiles[:slab_nt])
        wbufs = [C.sb("wb", [128, KT * totw], BF16) for _ in range(nb)]
    w_ap = W if isinstance(W, bass.AP) else W.ap()
    w_re = w_ap.rearrange("(kt p) n -> p kt n", p=128)
    slab_loaded = [None] * slabs
    slab_free_cnt = [None] * slabs
    entry_wf = C.s_wf.n

    def load_slab(s):
        lo = s * slab_nt
        hi = min(NTT, lo + slab_nt)
        buf = s % len(wbufs)
        waits = [(C.s_wf, entry_wf)] if s < 2 else \
            [(C.s_wf, slab_free_cnt[s - 2])]
        c0 = n_tiles[lo][0]
        c1 = n_tiles[hi - 1][0] + n_tiles[hi - 1][1]
        sl = C.s_wl[s % 2]
        p.step("gpsimd",
               lambda E, buf=buf, c0=c0, c1=c1: E.dma_start(
                   out=wbufs[buf][:, 0:KT * (c1 - c0)],
                   in_=w_re[:, :, c0:c1]),
               waits=waits, inc=sl, inc_k=16)
        slab_loaded[s] = (sl, sl.n)

    load_slab(0)
    if slabs > 1:
        load_slab(1)

    ev_sem = C.s_sc if evac_eng == "scalar" else C.s_ve

    def do_tile(s, ntl, tc, bank_i):
        lo = s * slab_nt
        hi = min(NTT, lo + slab_nt)
        buf = s % len(wbufs)
        c0 = n_tiles[lo][0]
        cw = n_tiles[hi - 1][0] + n_tiles[hi - 1][1] - c0
        nti = lo + ntl
        ofs, width = n_tiles[nti]
        ps = C.B[bank_i]
        waits = [slab_loaded[s]] + C.bank_acquire(bank_i)
        t0 = tok0 + tc * TC
        for kt in range(KT):
            def fmm(E, buf=buf, kt=kt, t0=t0, ps=ps, c0=c0, cw=cw,
                    ofs=ofs, width=width):
                return E.matmul(
                    ps[0:width, :],
                    wbufs[buf][:, kt * cw + (ofs - c0):
                               kt * cw + (ofs - c0) + width],
                    x_sb[:, x_base + kt * x_Ttot + t0:
                         x_base + kt * x_Ttot + t0 + TC],
                    start=(kt == 0), stop=(kt == KT - 1))
            p.step("tensor", fmm, waits=waits if kt == 0 else (),
                   inc=C.s_mm if kt == KT - 1 else None)
        mm_done = C.s_mm.n
        p.step(evac_eng,
               lambda E, ps=ps, nti=nti, t0=t0: evac(E, ps, nti, t0),
               waits=[(C.s_mm, mm_done)], inc=ev_sem)
        C.note_bank(bank_i, ev_sem)

    bi = 0
    for s in range(slabs):
        lo = s * slab_nt
        hi = min(NTT, lo + slab_nt)
        if tc_outer:
            for tc in range(n_tc):
                for ntl in range(hi - lo):
                    do_tile(s, ntl, tc, banks[bi % len(banks)])
                    bi += 1
                if chunk_hook is not None:
                    chunk_hook(tc, tok0 + tc * TC)
        else:
            for ntl in range(hi - lo):
                for tc in range(n_tc):
                    do_tile(s, ntl, tc, banks[bi % len(banks)])
                    bi += 1
        p.step(evac_eng, lambda E: E.nop(), inc=C.s_wf)
        slab_free_cnt[s] = C.s_wf.n
        if s + 2 < slabs:
            load_slab(s + 2)



class LNState:

    def __init__(self, C):
        self.rows = C.sb("lnrows", [1, 2560], F32)
        self.sq = C.sb("lnsq", [128, DT * TC], BF16)
        self.xcb = C.sb("lnxcb", [128, DT * TC], BF16)
        self.sq_busy = None


def ln_chunk(C, L, src_bf, src_ofs, src_stride, *, rms, eps,
             fence_in=()):
    p = C.p

    def sl(kt):
        return slice(src_ofs + kt * src_stride, src_ofs + kt * src_stride + TC)
    sq_w = [L.sq_busy] if L.sq_busy else []
    for kt in range(DT):
        p.step("scalar",
               lambda E, kt=kt: E.activation(
                   L.sq[:, kt * TC:(kt + 1) * TC], src_bf[:, sl(kt)],
                   AF.Square),
               waits=(list(fence_in) + sq_w) if kt == 0 else (),
               inc=C.s_sc if kt == DT - 1 else None)
    sq_done = C.s_sc.n
    wa = C.bank_acquire(3)
    wb = C.bank_acquire(4)
    if not rms:
        for kt in range(DT):
            p.step("tensor",
                   lambda E, kt=kt: E.matmul(
                       C.B[3][0:1, :], C.ones_col[:, :], src_bf[:, sl(kt)],
                       start=(kt == 0), stop=(kt == DT - 1)),
                   waits=(list(fence_in) + wa) if kt == 0 else (),
                   inc=C.s_mm if kt == DT - 1 else None)
    for kt in range(DT):
        p.step("tensor",
               lambda E, kt=kt: E.matmul(
                   C.B[4][0:1, :], C.ones_col[:, :],
                   L.sq[:, kt * TC:(kt + 1) * TC],
                   start=(kt == 0), stop=(kt == DT - 1)),
               waits=([(C.s_sc, sq_done)] + wb) if kt == 0 else (),
               inc=C.s_mm if kt == DT - 1 else None)
    L.sq_busy = (C.s_mm, C.s_mm.n)
    mm_done = C.s_mm.n
    rows = L.rows
    p.wait("vector", [(C.s_mm, mm_done), (C.s_sc, C.s_sc.n)])
    p.step("vector", lambda E: E.tensor_scalar_mul(
        rows[0:1, 512:1024], C.B[4][0:1, :], 1.0 / D), inc=C.s_ve)
    C.note_bank(4, C.s_ve)
    if rms:
        p.step("vector", lambda E: E.tensor_copy(
            rows[0:1, 1024:1536], rows[0:1, 512:1024]), inc=C.s_ve)
    else:
        p.step("vector", lambda E: E.tensor_scalar_mul(
            rows[0:1, 0:512], C.B[3][0:1, :], 1.0 / D), inc=C.s_ve)
        C.note_bank(3, C.s_ve)
        p.step("vector", lambda E: E.tensor_mul(
            rows[0:1, 1536:2048], rows[0:1, 0:512], rows[0:1, 0:512]))
        p.step("vector", lambda E: E.tensor_sub(
            rows[0:1, 1024:1536], rows[0:1, 512:1024],
            rows[0:1, 1536:2048]), inc=C.s_ve)
    var_done = C.s_ve.n
    p.wait("scalar", [(C.s_ve, var_done)])
    p.step("scalar", lambda E: E.activation(
        rows[0:1, 1536:2048], rows[0:1, 1024:1536], AF.Ln,
        bias=C.eps_ap(eps)[0:1, 0:1]))
    p.step("scalar", lambda E: E.activation(
        rows[0:1, 1536:2048], rows[0:1, 1536:2048], AF.Exp, scale=-0.5),
        inc=C.s_sc)
    rstd_done = C.s_sc.n
    if not rms:
        p.wait("vector", [(C.s_sc, rstd_done)])
        p.step("vector", lambda E: E.scalar_tensor_tensor(
            rows[0:1, 2048:2560], rows[0:1, 0:512], -1.0,
            rows[0:1, 1536:2048], ALU.mult, ALU.mult), inc=C.s_ve)
    rowf = [(C.s_sc, rstd_done), (C.s_ve, C.s_ve.n)]
    p.step("tensor", lambda E: E.matmul(
        C.U[0][:, :], C.ones_row[0:1, :], rows[0:1, 1536:2048],
        start=True, stop=True),
        waits=rowf + C.bank_acquire(5), inc=C.s_mm)
    if not rms:
        p.step("tensor", lambda E: E.matmul(
            C.U[1][:, :], C.ones_row[0:1, :], rows[0:1, 2048:2560],
            start=True, stop=True),
            waits=C.bank_acquire(6), inc=C.s_mm)
    return [(C.s_mm, C.s_mm.n)]


def emit_pre_ln(C, L, x_dram, xn_sb, *, tok0=0, Tlen=T, xn_Ttot=None,
                rms=False, eps=LN_EPS, nw_sb=None, dram_ready=()):
    p = C.p
    if xn_Ttot is None:
        xn_Ttot = Tlen
    xcb = L.xcb
    x_re = x_dram.ap().rearrange("p (kt t) -> p kt t", kt=DT)
    nch = Tlen // TC
    apply_done = {}
    for i in range(nch):
        t0 = tok0 + i * TC
        waits = list(dram_ready) if i == 0 else \
            [(C.s_ve, apply_done[i - 1])]
        p.step("gpsimd", lambda E, t0=t0: E.dma_start(
            out=xcb[:, :], in_=x_re[:, :, t0:t0 + TC]),
            waits=waits, inc=C.s_w, inc_k=16)
        ld = C.s_w.n
        fence = ln_chunk(C, L, xcb, 0, TC, rms=rms, eps=eps,
                         fence_in=[(C.s_w, ld)])
        p.wait("vector", fence)
        for kt in range(DT):
            dst = slice(kt * xn_Ttot + i * TC, kt * xn_Ttot + i * TC + TC)
            last = C.s_ve if kt == DT - 1 else None
            if rms and nw_sb is not None:
                p.step("vector", lambda E, kt=kt, dst=dst:
                       E.scalar_tensor_tensor(
                           xn_sb[:, dst], xcb[:, kt * TC:(kt + 1) * TC],
                           nw_sb[:, kt:kt + 1], C.U[0][:, :],
                           ALU.mult, ALU.mult), inc=last)
            elif rms:
                p.step("vector", lambda E, kt=kt, dst=dst:
                       E.tensor_mul(
                           xn_sb[:, dst], xcb[:, kt * TC:(kt + 1) * TC],
                           C.U[0][:, :]), inc=last)
            else:
                p.step("vector", lambda E, kt=kt:
                       E.tensor_mul(
                           xcb[:, kt * TC:(kt + 1) * TC],
                           xcb[:, kt * TC:(kt + 1) * TC],
                           C.U[0][:, :]))
                p.step("vector", lambda E, kt=kt, dst=dst:
                       E.tensor_add(
                           xn_sb[:, dst], xcb[:, kt * TC:(kt + 1) * TC],
                           C.U[1][:, :]), inc=last)
        C.note_bank(5, C.s_ve)
        C.note_bank(6, C.s_ve)
        apply_done[i] = C.s_ve.n


def emit_post_ln_res(C, L, y_sb, yb_sb, x_dram_in, x_dram_out, *, t0,
                     y_ready, eps=LN_EPS, xc, xo=None):
    p = C.p
    x_re = x_dram_in.ap().rearrange("p (kt t) -> p kt t", kt=DT)
    xo_re = x_dram_out.ap().rearrange("p (kt t) -> p kt t", kt=DT)
    p.step("sync", lambda E: E.dma_start(
        out=xc[:, :], in_=x_re[:, :, t0:t0 + TC]),
        waits=[(C.s_st, C.s_st.n)], inc=C.s_x, inc_k=16)
    xld = C.s_x.n
    p.wait("vector", list(y_ready))
    p.step("vector", lambda E: E.tensor_copy(yb_sb[:, :], y_sb[:, :]),
           inc=C.s_ve)
    yb_done = C.s_ve.n
    fence = ln_chunk(C, L, yb_sb, 0, TC, rms=False, eps=eps,
                     fence_in=[(C.s_ve, yb_done)])
    p.wait("vector", fence + [(C.s_x, xld)])
    for kt in range(DT):
        p.step("vector", lambda E, kt=kt: E.tensor_mul(
            y_sb[:, kt * TC:(kt + 1) * TC],
            y_sb[:, kt * TC:(kt + 1) * TC], C.U[0][:, :]))
        p.step("vector", lambda E, kt=kt: E.tensor_add(
            xc[:, kt * TC:(kt + 1) * TC],
            xc[:, kt * TC:(kt + 1) * TC], C.U[1][:, :]))
        p.step("vector", lambda E, kt=kt: E.tensor_add(
            xc[:, kt * TC:(kt + 1) * TC],
            y_sb[:, kt * TC:(kt + 1) * TC],
            xc[:, kt * TC:(kt + 1) * TC]),
            inc=C.s_ve if kt == DT - 1 else None)
    C.note_bank(5, C.s_ve)
    C.note_bank(6, C.s_ve)
    out_done = C.s_ve.n
    p.step("sync", lambda E: E.dma_start(
        out=xo_re[:, :, t0:t0 + TC], in_=xc[:, :]),
        waits=[(C.s_ve, out_done)], inc=C.s_st, inc_k=16)
    return C.s_st.n, out_done



def emit_attention_b(C, *, q_sb, q_Ttot, kd_sb, kd_base, vT_sb, n_hp,
                     exp_scale=1.0, softcap=None, causal_t0=None,
                     n_st=16, p_sl=None, rr=None, rbc=None, ctx_writer=None,
                     Tlen=S, kv_of_hp=None, masks=None, sc_tmp=None,
                     pre_hp_hook=None, vt_h_stride=0):
    p = C.p
    n_tc = Tlen // TC
    av_done = {}
    av_by_slot = {}
    for hp in range(n_hp):
        kcb, vtb = (0, 0) if kv_of_hp is None else kv_of_hp(hp)
        if pre_hp_hook is not None:
            pre_hp_hook(hp)
        for tcj in range(n_tc):
            st_list = list(range(n_st))
            if causal_t0 is not None:
                tq0 = tcj * TC
                st_list = [st for st in range(n_st) if st * 128 <= tq0 + 511]
            first_av = [True, True]
            for sti, st in enumerate(st_list):
                for h in range(2):
                    sbank = (0, 2)[h] + sti % 2
                    ps = C.B[sbank]
                    w = C.bank_acquire(sbank)
                    p.step("tensor",
                           lambda E, h=h, st=st, hp=hp, tcj=tcj, ps=ps,
                           kcb=kcb: E.matmul(
                               ps[:, :],
                               kd_sb[h * 64:(h + 1) * 64,
                                     kd_base + kcb + st * 128:
                                     kd_base + kcb + st * 128 + 128],
                               q_sb[h * 64:(h + 1) * 64,
                                    hp * q_Ttot + tcj * TC:
                                    hp * q_Ttot + tcj * TC + TC],
                               start=True, stop=True,
                               tile_position=(h * 64, 0)),
                           waits=w, inc=C.s_mm)
                    qk_done = C.s_mm.n
                    slot = p_sl[h * 2 + sti % 2]
                    ew = [(C.s_mm, qk_done)]
                    si = h * 2 + sti % 2
                    if si in av_by_slot:
                        ew.append((C.s_mm, av_by_slot[si]))
                    if softcap is None:
                        p.step("scalar",
                               lambda E, ps=ps, slot=slot: E.activation(
                                   slot[:, :], ps[:, :], AF.Exp,
                                   scale=float(exp_scale)),
                               waits=ew, inc=C.s_sc)
                    else:
                        p.step("scalar",
                               lambda E, ps=ps, slot=slot: E.activation(
                                   slot[:, :], ps[:, :], AF.Tanh,
                                   scale=1.0 / 50.0),
                               waits=ew)
                        p.step("scalar",
                               lambda E, slot=slot: E.activation(
                                   slot[:, :], slot[:, :], AF.Exp,
                                   scale=6.25),
                               inc=C.s_sc)
                    exp_done = C.s_sc.n
                    C.note_bank(sbank, C.s_sc)
                    pr_sem, pr_n = C.s_sc, exp_done
                    if causal_t0 is not None:
                        tq0 = tcj * TC
                        diff = st * 128 - tq0
                        if diff > -128:
                            mk = masks[diff]
                            p.step("vector",
                                   lambda E, slot=slot, mk=mk: E.tensor_mul(
                                       slot[:, :], slot[:, :], mk[:, :]),
                                   waits=[(C.s_sc, exp_done)], inc=C.s_ve)
                            pr_sem, pr_n = C.s_ve, C.s_ve.n
                    avb = 5 + h
                    aw = [(pr_sem, pr_n)]
                    if first_av[h]:
                        aw += C.bank_acquire(avb)
                        first_av[h] = False
                    vo = vtb + h * vt_h_stride
                    p.step("tensor",
                           lambda E, st=st, slot=slot, h=h, vo=vo,
                           first=(sti == 0), last=(sti == len(st_list) - 1):
                           E.matmul(
                               C.U[h][0:65, :],
                               vT_sb[:, vo + st * 65: vo + st * 65 + 65],
                               slot[:, :], start=first, stop=last,
                               skip_group_check=True),
                           waits=aw, inc=C.s_mm)
                    av_done[(h, sti)] = C.s_mm.n
                    av_by_slot[si] = C.s_mm.n
            for h in range(2):
                p.wait("vector", [(C.s_mm, av_done[(h, len(st_list) - 1)])])
                p.step("vector", lambda E, h=h: E.reciprocal(
                    rr[h][0:1, 0:512], C.U[h][64:65, :]), inc=C.s_ve)
                rcp_done = C.s_ve.n
                p.step("tensor", lambda E, h=h: E.matmul(
                    C.B[4][0:64, :], C.ones_row[0:1, 0:64],
                    rr[h][0:1, 0:512], start=True, stop=True),
                    waits=[(C.s_ve, rcp_done)] + C.bank_acquire(4),
                    inc=C.s_mm)
                bc_done = C.s_mm.n
                p.step("vector", lambda E, h=h: E.tensor_copy(
                    rbc[h][0:64, :], C.B[4][0:64, :]),
                    waits=[(C.s_mm, bc_done)], inc=C.s_ve)
                C.note_bank(4, C.s_ve)
                cp_done = C.s_ve.n
                p.step("vector",
                       lambda E, h=h, hp=hp, tcj=tcj:
                       ctx_writer(E, h, hp, tcj,
                                  C.U[h][0:64, :], rbc[h][0:64, :]),
                       waits=[(C.s_ve, cp_done)], inc=C.s_ve)
                C.note_bank(5 + h, C.s_ve)


def emit_vT_build(C, v_sb, v_rows, vT_sb, vT_base, n_st, *, src_col0=0,
                  fence=()):
    p = C.p
    r0, r1 = v_rows
    for st in range(n_st):
        w = C.bank_acquire(7)
        p.step("tensor",
               lambda E, st=st, r0=r0, r1=r1: E.transpose(
                   C.TP[:, 0:r1 - r0],
                   v_sb[r0:r1, src_col0 + st * 128: src_col0 + st * 128 + 128],
                   C.identity[r0:r1, r0:r1]),
               waits=list(fence) + w, inc=C.s_mm)
        td = C.s_mm.n
        p.step("vector",
               lambda E, st=st, r1=r1, r0=r0: E.tensor_copy(
                   vT_sb[:, vT_base + st * 65: vT_base + st * 65 + (r1 - r0)],
                   C.TP[:, 0:r1 - r0]),
               waits=[(C.s_mm, td)], inc=C.s_ve)
        C.note_bank(7, C.s_ve)


DBG = False
DBGH = False


def dbg_dump_dram(C, name, dram):
    if not DBGH:
        return
    d = C.nc.dram_tensor("dbgh_" + name, list(dram.shape), dram.dtype,
                         kind="ExternalOutput")
    C.join()
    C.p.step("sync", lambda E: E.dma_start(out=d.ap(), in_=dram.ap()),
             inc=C.s_st, inc_k=16)
    C.join()


def dbg_dump(C, name, sb, waits):
    if not DBG:
        return
    d = C.nc.dram_tensor("dbg_" + name, list(sb.shape),
                         sb.dtype if hasattr(sb, "dtype") else F32,
                         kind="ExternalOutput")
    C.join()
    C.p.step("sync", lambda E: E.dma_start(out=d.ap(), in_=sb[:, :]),
             waits=waits, inc=C.s_st, inc_k=16)
    C.join()



def build_l0_attn(C):
    nc = C.nc
    p = C.p
    x_in = nc.dram_tensor("x_in", [128, DT * T], F32, kind="ExternalInput")
    x_out = nc.dram_tensor("x_out", [128, DT * T], F32,
                           kind="ExternalOutput")
    wq = nc.dram_tensor("wq", [D, D], F32, kind="ExternalInput")
    bq = nc.dram_tensor("bq", [128, 8], F32, kind="ExternalInput")
    wk = nc.dram_tensor("wk", [D, 64], F32, kind="ExternalInput")
    wv = nc.dram_tensor("wv", [D, 64], F32, kind="ExternalInput")
    bv = nc.dram_tensor("bv", [128, 1], F32, kind="ExternalInput")
    wo = nc.dram_tensor("wo", [D, D], F32, kind="ExternalInput")
    bo = nc.dram_tensor("bo", [128, 8], F32, kind="ExternalInput")

    C.setup_consts()
    L = LNState(C)
    xn = C.sb("xn", [128, DT * S], BF16)
    qc = C.sb("qc", [128, DT * S], BF16)
    kd = C.sb("kd", [128, S], BF16)
    vv = C.sb("vv", [128, S], BF16)
    vT = C.sb("vT", [128, 16 * 65], BF16)
    p_sl = [C.sb("psl", [128, 512], BF16) for _ in range(4)]
    rr = [C.sb("rr", [1, 512], F32) for _ in range(2)]
    rbc = [C.sb("rbc", [64, 512], BF16) for _ in range(2)]
    ww = [C.sb("ww", [128, DT * 4 * 128], BF16) for _ in range(2)]
    wwo = [C.sb("wwo", [128, DT * DT * 128], BF16)]
    y = C.sb("y", [128, DT * TC], F32)
    yb = C.sb("yb", [128, DT * TC], BF16)
    xc = C.sb("xc", [128, DT * TC], F32)
    bq_s = C.sb("bq_s", [128, 8], F32)
    bv_s = C.sb("bv_s", [128, 1], F32)
    bo_s = C.sb("bo_s", [128, 8], F32)

    for dram, sb_ in ((bq, bq_s), (bv, bv_s), (bo, bo_s)):
        p.step("sync", lambda E, d=dram, s=sb_: E.dma_start(
            out=s[:, :], in_=d.ap()), inc=C.s_x, inc_k=16)
    bld = C.s_x.n
    p.step("vector", lambda E: E.tensor_scalar_mul(
        bq_s[:, :], bq_s[:, :], 0.125), waits=[(C.s_x, bld)],
        inc=C.s_ve)
    bias_ready = [(C.s_ve, C.s_ve.n), (C.s_x, bld)]

    prev_war = []
    for b in range(BLOC):
        tok0 = b * S
        emit_pre_ln(C, L, x_in, xn, tok0=tok0, Tlen=S, xn_Ttot=S,
                    rms=False, dram_ready=C.const_ready)
        pre_done = [(C.s_ve, C.s_ve.n)]
        p.wait("tensor", pre_done)
        if b == 0:
            dbg_dump(C, "xn", xn, pre_done)
        p.wait("scalar", bias_ready)

        def ev_q(E, ps, nti, t0):
            return E.activation(
                qc[:, nti * S + t0: nti * S + t0 + TC], ps[:, :],
                AF.Identity, bias=bq_s[:, nti:nti + 1], scale=0.125)
        linear_layer(C, x_sb=xn, KT=DT, W=wq, n_tiles=DT, evac=ev_q,
                     x_Ttot=S, slab_nt=4, wbufs=ww)

        def ev_k(E, ps, nti, t0):
            E.activation(kd[0:64, t0:t0 + TC], ps[0:64, :], AF.Identity)
            return E.activation(kd[64:128, t0:t0 + TC], ps[0:64, :],
                                AF.Identity)
        linear_layer(C, x_sb=xn, KT=DT, W=wk, n_tiles=[(0, 64)], evac=ev_k,
                     x_Ttot=S, slab_nt=1, wbufs=ww)

        def ev_v(E, ps, nti, t0):
            return E.activation(vv[0:64, t0:t0 + TC], ps[0:64, :],
                                AF.Identity, bias=bv_s[0:64, 0:1], scale=1.0)
        linear_layer(C, x_sb=xn, KT=DT, W=wv, n_tiles=[(0, 64)], evac=ev_v,
                     x_Ttot=S, slab_nt=1, wbufs=ww)
        kv_done = [(C.s_sc, C.s_sc.n)]
        if b == 0:
            dbg_dump(C, "q", qc, kv_done)
            dbg_dump(C, "kd", kd, kv_done)
            dbg_dump(C, "vv", vv, kv_done)
        p.step("vector", lambda E: E.memset(vT[:, :], 1.0),
               waits=[(C.s_ve, C.s_ve.n)], inc=C.s_ve)
        ones_done = C.s_ve.n
        emit_vT_build(C, vv, (0, 64), vT, 0, 16,
                      fence=kv_done + [(C.s_ve, ones_done)])
        p.wait("tensor", [(C.s_sc, C.s_sc.n), (C.s_ve, C.s_ve.n)])

        def ctx_w(E, h, hp, tcj, av_ps, rb):
            return E.tensor_mul(
                xn[h * 64:(h + 1) * 64,
                   hp * S + tcj * TC: hp * S + tcj * TC + TC],
                av_ps, rb)
        emit_attention_b(C, q_sb=qc, q_Ttot=S, kd_sb=kd, kd_base=0,
                         vT_sb=vT, n_hp=DT, exp_scale=1.0, p_sl=p_sl,
                         rr=rr, rbc=rbc, ctx_writer=ctx_w)
        if b == 0:
            dbg_dump(C, "vT", vT, [(C.s_ve, C.s_ve.n)])
            dbg_dump(C, "ctx", xn, [(C.s_ve, C.s_ve.n)])
        p.wait("scalar", [(C.s_ve, C.s_ve.n)])
        p.wait("tensor", [(C.s_ve, C.s_ve.n)])
        state = {"war": list(prev_war)}

        def ev_o(E, ps, nti, t0):
            return E.activation(
                y[:, nti * TC: nti * TC + TC], ps[:, :], AF.Identity,
                bias=bo_s[:, nti:nti + 1], scale=1.0)

        def hook(tc, t0local, state=state, tok0=tok0):
            yr = [(C.s_sc, C.s_sc.n)]
            stc, od = emit_post_ln_res(
                C, L, y, yb, x_in, x_out, t0=tok0 + t0local,
                y_ready=yr + state["war"], xc=xc)
            state["war"] = [(C.s_ve, od)]
            p.wait("scalar", [(C.s_ve, od)])
            p.wait("sync", [(C.s_ve, od)])
            p.wait("gpsimd", [(C.s_ve, od)])
        linear_layer(C, x_sb=xn, KT=DT, W=wo, n_tiles=DT, evac=ev_o,
                     x_Ttot=S, slab_nt=DT, wbufs=wwo, tc_outer=True,
                     chunk_hook=hook)
        prev_war = state["war"]
    C.join()


def prep_l0_attn(inp, xT_list):
    shared = {
        "wq": np.ascontiguousarray(np.asarray(inp["L0_attn_Wq"]).T),
        "bq": to_cols(inp["L0_attn_bq"]),
        "wk": np.ascontiguousarray(np.asarray(inp["L0_attn_Wk"]).T),
        "wv": np.ascontiguousarray(np.asarray(inp["L0_attn_Wv"]).T),
        "bv": np.concatenate([np.asarray(inp["L0_attn_bv"], np.float32),
                              np.zeros(64, np.float32)]).reshape(128, 1),
        "wo": np.ascontiguousarray(np.asarray(inp["L0_attn_Wo"]).T),
        "bo": to_cols(inp["L0_attn_bo"]),
        "c_identity": np.eye(128, dtype=np.float32),
    }
    return [dict(shared, x_in=xT_list[c]) for c in range(NCORES)]



def build_l2_attn(C):
    nc = C.nc
    p = C.p
    x_in = nc.dram_tensor("x_in", [128, DT * T], F32, kind="ExternalInput")
    x_out = nc.dram_tensor("x_out", [128, DT * T], F32,
                           kind="ExternalOutput")
    wq = nc.dram_tensor("wq", [D, D], F32, kind="ExternalInput")
    wkv = nc.dram_tensor("wkv", [D, 2 * D], F32, kind="ExternalInput")
    wo = nc.dram_tensor("wo", [D, D], F32, kind="ExternalInput")
    nw = nc.dram_tensor("nw", [128, 8], F32, kind="ExternalInput")
    qg = nc.dram_tensor("qg", [128, 8], F32, kind="ExternalInput")
    kg = nc.dram_tensor("kg", [128, 8], F32, kind="ExternalInput")

    C.setup_consts()
    L = LNState(C)
    xn = C.sb("xn", [128, DT * S], BF16)
    qs = C.sb("qs", [128, DT * S], BF16)
    ks = C.sb("ks", [128, DT * S], BF16)
    vs = C.sb("vs", [128, DT * S], BF16)
    vT = C.sb("vT", [128, 2 * 16 * 65], BF16)
    p_sl = [C.sb("psl", [128, 512], BF16) for _ in range(4)]
    rr = [C.sb("rr", [1, 512], F32) for _ in range(2)]
    rbc0 = C.sb("rbc", [64, 512], BF16)
    rbc = [rbc0, rbc0]
    wwbig = C.sb("wwbig", [128, DT * DT * 128], BF16)
    ww = [wwbig[:, 0:DT * 4 * 128], wwbig[:, DT * 4 * 128:DT * 8 * 128]]
    wwo = [wwbig[:, :]]
    qs32 = qs[:, :].bitcast(F32)
    y = qs32[:, 0:DT * TC]
    yb = L.xcb
    xc = qs32[:, DT * TC:2 * DT * TC]
    nw_s = C.sb("nw_s", [128, 8], F32)
    qg_s = C.sb("qg_s", [128, 8], F32)
    kg_s = C.sb("kg_s", [128, 8], F32)
    onesh = C.sb("onesh", [128, 2], BF16)
    onebc = C.sb("onebc", [2, 128], F32)
    sqh = p_sl[3]
    nr = C.sb("nr", [2, 512], F32)

    for dram, sb_ in ((nw, nw_s), (qg, qg_s), (kg, kg_s)):
        p.step("sync", lambda E, d=dram, s=sb_: E.dma_start(
            out=s[:, :], in_=d.ap()), inc=C.s_x, inc_k=16)
    bld = C.s_x.n
    p.wait("vector", [(C.s_x, bld)])
    for g in (qg_s, kg_s):
        p.step("vector", lambda E, g=g: E.tensor_scalar(
            g[:, :], g[:, :], 1.0, 8.0, ALU.add, ALU.mult), inc=C.s_ve)
    p.step("vector", lambda E: E.memset(onesh[:, :], 0.0))
    p.step("vector", lambda E: E.memset(onesh[0:64, 0:1], 1.0))
    p.step("vector", lambda E: E.memset(onesh[64:128, 1:2], 1.0),
           inc=C.s_ve)
    onebc_d = nc.dram_tensor("c_onebc", [2, 128], F32, kind="ExternalInput")
    p.step("sync", lambda E: E.dma_start(out=onebc[:, :], in_=onebc_d.ap()),
           inc=C.s_x, inc_k=16)
    consts2 = [(C.s_ve, C.s_ve.n), (C.s_x, C.s_x.n)]
    p.wait("tensor", consts2)
    p.wait("scalar", consts2)

    prev_war = []
    for b in range(BLOC):
        tok0 = b * S
        emit_pre_ln(C, L, x_in, xn, tok0=tok0, Tlen=S, xn_Ttot=S,
                    rms=True, eps=1.1920929e-07, nw_sb=nw_s,
                    dram_ready=C.const_ready)
        pre_done = [(C.s_ve, C.s_ve.n)]
        p.wait("tensor", pre_done)

        def ev_pl(dst):
            def ev(E, ps, nti, t0, dst=dst):
                return E.activation(
                    dst[:, nti * S + t0: nti * S + t0 + TC], ps[:, :],
                    AF.Identity)
            return ev
        linear_layer(C, x_sb=xn, KT=DT, W=wq, n_tiles=DT, evac=ev_pl(qs),
                     x_Ttot=S, slab_nt=4, wbufs=ww)
        linear_layer(C, x_sb=xn, KT=DT, W=wkv,
                     n_tiles=[(i * 128, 128) for i in range(8)],
                     evac=ev_pl(ks), x_Ttot=S, slab_nt=4, wbufs=ww)
        linear_layer(C, x_sb=xn, KT=DT, W=wkv,
                     n_tiles=[(1024 + i * 128, 128) for i in range(8)],
                     evac=ev_pl(vs), x_Ttot=S, slab_nt=4, wbufs=ww)
        kv_done = [(C.s_sc, C.s_sc.n)]
        if b == 0:
            dbg_dump(C, "xn2", xn, [(C.s_ve, C.s_ve.n)])
            dbg_dump(C, "q_raw", qs, kv_done)
            dbg_dump(C, "k_raw", ks, kv_done)
            dbg_dump(C, "v_raw", vs, kv_done)
        p.wait("scalar", kv_done)
        p.wait("tensor", kv_done)
        for src, gsc in ((qs, qg_s), (ks, kg_s)):
            for nt in range(DT):
                for tc in range(S // TC):
                    sl = slice(nt * S + tc * TC, nt * S + tc * TC + TC)
                    p.step("scalar", lambda E, src=src, sl=sl: E.activation(
                        sqh[:, :], src[:, sl], AF.Square), inc=C.s_sc)
                    sqd = C.s_sc.n
                    wb4 = C.bank_acquire(4)
                    p.step("tensor", lambda E: E.matmul(
                        C.B[4][0:2, :], onesh[:, :], sqh[:, :],
                        start=True, stop=True),
                        waits=[(C.s_sc, sqd)] + wb4, inc=C.s_mm)
                    mmd = C.s_mm.n
                    p.wait("scalar", [(C.s_mm, mmd)])
                    p.step("scalar", lambda E: E.activation(
                        nr[0:2, :], C.B[4][0:2, :], AF.Ln), inc=C.s_sc)
                    C.note_bank(4, C.s_sc)
                    p.step("scalar", lambda E: E.activation(
                        nr[0:2, :], nr[0:2, :], AF.Exp, scale=-0.5),
                        inc=C.s_sc)
                    nrd = C.s_sc.n
                    wb5 = C.bank_acquire(5)
                    p.step("tensor", lambda E: E.matmul(
                        C.U[0][:, :], onebc[:, :], nr[0:2, :],
                        start=True, stop=True),
                        waits=[(C.s_sc, nrd)] + wb5, inc=C.s_mm)
                    bcd = C.s_mm.n
                    p.wait("vector", [(C.s_mm, bcd)])
                    p.step("vector", lambda E, src=src, sl=sl: E.tensor_mul(
                        src[:, sl], src[:, sl], C.U[0][:, :]))
                    p.step("vector", lambda E, src=src, sl=sl, gsc=gsc,
                           nt=nt: E.tensor_scalar(
                               src[:, sl], src[:, sl], gsc[:, nt:nt + 1],
                               None, ALU.mult), inc=C.s_ve)
                    C.note_bank(5, C.s_ve)
        norm_done = [(C.s_ve, C.s_ve.n)]
        if b == 0:
            dbg_dump(C, "qn", qs, norm_done)
            dbg_dump(C, "kn", ks, norm_done)
        p.wait("tensor", norm_done)
        p.wait("scalar", norm_done)

        def pre_hp(hp):
            p.wait("vector", [(C.s_mm, C.s_mm.n)])
            p.step("vector", lambda E: E.memset(vT[:, :], 1.0), inc=C.s_ve)
            od = C.s_ve.n
            for h in range(2):
                emit_vT_build(C, vs, (h * 64, (h + 1) * 64), vT, h * 16 * 65,
                              16, src_col0=hp * S,
                              fence=[(C.s_ve, od)])
            p.wait("tensor", [(C.s_ve, C.s_ve.n)])

        def ctx_w(E, h, hp, tcj, av_ps, rb):
            return E.tensor_mul(
                xn[h * 64:(h + 1) * 64,
                   hp * S + tcj * TC: hp * S + tcj * TC + TC],
                av_ps, rb)
        emit_attention_b(C, q_sb=qs, q_Ttot=S, kd_sb=ks, kd_base=0,
                         vT_sb=vT, n_hp=DT, exp_scale=1.0,
                         softcap=True, p_sl=p_sl,
                         rr=rr, rbc=rbc, ctx_writer=ctx_w,
                         kv_of_hp=lambda hp: (hp * S, 0),
                         pre_hp_hook=pre_hp, vt_h_stride=16 * 65)
        if b == 0:
            dbg_dump(C, "ctx2", xn, [(C.s_ve, C.s_ve.n)])
        p.wait("scalar", [(C.s_ve, C.s_ve.n)])
        p.wait("tensor", [(C.s_ve, C.s_ve.n)])
        p.wait("sync", [(C.s_ve, C.s_ve.n), (C.s_mm, C.s_mm.n)])
        state = {"war": list(prev_war)}

        def ev_o(E, ps, nti, t0):
            return E.activation(
                y[:, nti * TC: nti * TC + TC], ps[:, :], AF.Identity)

        def hook(tc, t0local, state=state, tok0=tok0):
            yr = [(C.s_sc, C.s_sc.n)]
            stc, od = emit_post_ln_res(
                C, L, y, yb, x_in, x_out, t0=tok0 + t0local,
                y_ready=yr + state["war"], xc=xc)
            state["war"] = [(C.s_ve, od)]
            p.wait("scalar", [(C.s_ve, od)])
            p.wait("sync", [(C.s_ve, od)])
            p.wait("gpsimd", [(C.s_ve, od)])
        linear_layer(C, x_sb=xn, KT=DT, W=wo, n_tiles=DT, evac=ev_o,
                     x_Ttot=S, slab_nt=DT, wbufs=wwo, tc_outer=True,
                     chunk_hook=hook)
        prev_war = state["war"]
    C.join()


def prep_l2_attn(inp, xT_list):
    qg = (np.asarray(inp["L2_attn_q_gamma"], np.float32).reshape(D))
    kg = (np.asarray(inp["L2_attn_k_gamma"], np.float32).reshape(D))
    shared = {
        "wq": np.ascontiguousarray(np.asarray(inp["L2_attn_Wq"], np.float32)),
        "wkv": np.ascontiguousarray(np.asarray(inp["L2_attn_Wkv"],
                                               np.float32)),
        "wo": np.ascontiguousarray(np.asarray(inp["L2_attn_Wo"], np.float32)),
        "nw": to_cols(inp["L2_attn_norm_w"]),
        "qg": to_cols(qg),
        "kg": to_cols(kg),
        "c_identity": np.eye(128, dtype=np.float32),
        "c_onebc": np.kron(np.eye(2, dtype=np.float32),
                           np.ones((1, 64), np.float32)).reshape(2, 128),
    }
    return [dict(shared, x_in=xT_list[c]) for c in range(NCORES)]



def build_tern_ffn(C):
    nc = C.nc
    p = C.p
    x_in = nc.dram_tensor("x_in", [128, DT * T], F32, kind="ExternalInput")
    x_out = nc.dram_tensor("x_out", [128, DT * T], F32,
                           kind="ExternalOutput")
    w1 = nc.dram_tensor("w1", [D, 4096], F32, kind="ExternalInput")
    w2 = nc.dram_tensor("w2", [D, 4096], F32, kind="ExternalInput")
    h1 = nc.dram_tensor("h1", [4096, 4096], F32, kind="ExternalInput")
    h2 = nc.dram_tensor("h2", [4096, 4096], F32, kind="ExternalInput")
    w3 = nc.dram_tensor("w3", [4096, D], F32, kind="ExternalInput")
    s1 = nc.dram_tensor("s1", [128, 32], F32, kind="ExternalInput")
    s2 = nc.dram_tensor("s2", [128, 32], F32, kind="ExternalInput")
    hs1 = nc.dram_tensor("hs1", [128, 32], F32, kind="ExternalInput")
    hs2 = nc.dram_tensor("hs2", [128, 32], F32, kind="ExternalInput")
    s3 = nc.dram_tensor("s3", [128, 8], F32, kind="ExternalInput")

    C.setup_consts()
    L = LNState(C)
    TB = 512
    HT = 32
    xnc = C.sb("xnc", [128, DT * TB], BF16)
    a_sb = C.sb("a", [128, HT * TB], BF16)
    u_sb = C.sb("u", [128, HT * TB], BF16)
    h_sb = C.sb("h", [128, HT * TB], BF16)
    ww = [C.sb("ww", [128, HT * 2 * 128], BF16) for _ in range(2)]
    y = C.sb("y", [128, DT * TC], F32)
    yb = C.sb("yb", [128, DT * TC], BF16)
    xc = C.sb("xc", [128, DT * TC], F32)
    scales = {}
    for nm, dram in (("s1", s1), ("s2", s2), ("hs1", hs1), ("hs2", hs2),
                     ("s3", s3)):
        sb_ = C.sb(nm, [128, dram.shape[1]], F32)
        scales[nm] = sb_
        p.step("sync", lambda E, d=dram, s=sb_: E.dma_start(
            out=s[:, :], in_=d.ap()), inc=C.s_x, inc_k=16)
    p.wait("scalar", [(C.s_x, C.s_x.n)])
    p.wait("vector", [(C.s_x, C.s_x.n)])

    prev_war = []
    for oc in range(T // TB):
        tok0 = oc * TB
        emit_pre_ln(C, L, x_in, xnc, tok0=tok0, Tlen=TB, xn_Ttot=TB,
                    rms=False, dram_ready=C.const_ready)
        p.wait("tensor", [(C.s_ve, C.s_ve.n)])

        def ev_a(E, ps, nti, t0):
            return E.activation(
                a_sb[:, nti * TB + t0: nti * TB + t0 + TC], ps[:, :],
                AF.Silu, scale=scales["s1"][:, nti:nti + 1])
        linear_layer(C, x_sb=xnc, KT=DT, W=w1, n_tiles=HT, evac=ev_a,
                     x_Ttot=TB, slab_nt=2, wbufs=ww)
        p.wait("vector", [(C.s_sc, C.s_sc.n)])

        def ev_u(E, ps, nti, t0):
            return E.scalar_tensor_tensor(
                u_sb[:, nti * TB + t0: nti * TB + t0 + TC], ps[:, :],
                scales["s2"][:, nti:nti + 1],
                a_sb[:, nti * TB + t0: nti * TB + t0 + TC],
                ALU.mult, ALU.mult)
        linear_layer(C, x_sb=xnc, KT=DT, W=w2, n_tiles=HT, evac=ev_u,
                     evac_eng="vector", x_Ttot=TB, slab_nt=2, wbufs=ww)
        p.wait("tensor", [(C.s_ve, C.s_ve.n)])

        def ev_h(E, ps, nti, t0):
            return E.activation(
                h_sb[:, nti * TB + t0: nti * TB + t0 + TC], ps[:, :],
                AF.Gelu, scale=scales["hs1"][:, nti:nti + 1])
        linear_layer(C, x_sb=u_sb, KT=HT, W=h1, n_tiles=HT, evac=ev_h,
                     x_Ttot=TB, slab_nt=2, wbufs=ww)
        p.wait("tensor", [(C.s_sc, C.s_sc.n)])

        def ev_h2(E, ps, nti, t0):
            return E.activation(
                u_sb[:, nti * TB + t0: nti * TB + t0 + TC], ps[:, :],
                AF.Gelu, scale=scales["hs2"][:, nti:nti + 1])
        linear_layer(C, x_sb=h_sb, KT=HT, W=h2, n_tiles=HT, evac=ev_h2,
                     x_Ttot=TB, slab_nt=2, wbufs=ww)
        p.wait("tensor", [(C.s_sc, C.s_sc.n)])

        def ev_y(E, ps, nti, t0):
            return E.activation(
                y[:, nti * TC: nti * TC + TC], ps[:, :],
                AF.Identity, scale=scales["s3"][:, nti:nti + 1])
        linear_layer(C, x_sb=u_sb, KT=HT, W=w3, n_tiles=DT, evac=ev_y,
                     x_Ttot=TB, slab_nt=2, wbufs=ww)
        yr = [(C.s_sc, C.s_sc.n)] + prev_war
        stc, od = emit_post_ln_res(C, L, y, yb, x_in, x_out, t0=tok0,
                                   y_ready=yr, xc=xc)
        prev_war = [(C.s_ve, od)]
        p.wait("scalar", [(C.s_ve, od)])
        p.wait("sync", [(C.s_ve, od)])
        p.wait("gpsimd", [(C.s_ve, od)])
    C.join()


_TERN_KEYS = {
    "L0_ffn": {
        "w1_w": "L0_ffn_w1_w", "w1_s": "L0_ffn_w1_s",
        "w2_w": "L0_ffn_w2_w", "w2_s": "L0_ffn_w2_s",
        "h1_w": "L0_ffn_h1_w", "h1_s": "L0_ffn_h1_s",
        "h2_w": "L0_ffn_h2_w", "h2_s": "L0_ffn_h2_s",
        "w3_w": "L0_ffn_w3_w", "w3_s": "L0_ffn_w3_s",
    },
    "L2_ffn": {
        "w1_w": "L2_ffn_w1_w", "w1_s": "L2_ffn_w1_s",
        "w2_w": "L2_ffn_w2_w", "w2_s": "L2_ffn_w2_s",
        "h1_w": "L2_ffn_h1_w", "h1_s": "L2_ffn_h1_s",
        "h2_w": "L2_ffn_h2_w", "h2_s": "L2_ffn_h2_s",
        "w3_w": "L2_ffn_w3_w", "w3_s": "L2_ffn_w3_s",
    },
}


def prep_tern_ffn(pref):
    def prep(inp, xT_list):
        g = lambda k: np.asarray(inp[_TERN_KEYS[pref][k]], np.float32)
        shared = {
            "w1": np.ascontiguousarray(g("w1_w").T),
            "w2": np.ascontiguousarray(g("w2_w").T),
            "h1": np.ascontiguousarray(g("h1_w").T),
            "h2": np.ascontiguousarray(g("h2_w").T),
            "w3": np.ascontiguousarray(g("w3_w").T),
            "s1": to_cols(g("w1_s")),
            "s2": to_cols(g("w2_s")),
            "hs1": to_cols(g("h1_s")),
            "hs2": to_cols(g("h2_s")),
            "s3": to_cols(g("w3_s")),
            "c_identity": np.eye(128, dtype=np.float32),
        }
        return [dict(shared, x_in=xT_list[c]) for c in range(NCORES)]
    return prep



def build_l3_attn(C):
    nc = C.nc
    p = C.p
    x_in = nc.dram_tensor("x_in", [128, DT * T], F32, kind="ExternalInput")
    x_out = nc.dram_tensor("x_out", [128, DT * T], F32,
                           kind="ExternalOutput")
    wq = nc.dram_tensor("wq", [D, D], F32, kind="ExternalInput")
    bq = nc.dram_tensor("bq", [128, 8], F32, kind="ExternalInput")
    wk = nc.dram_tensor("wk", [D, 256], F32, kind="ExternalInput")
    bk = nc.dram_tensor("bk", [128, 2], F32, kind="ExternalInput")
    wv = nc.dram_tensor("wv", [D, 256], F32, kind="ExternalInput")
    bv = nc.dram_tensor("bv", [128, 2], F32, kind="ExternalInput")
    wo = nc.dram_tensor("wo", [D, D], F32, kind="ExternalInput")
    bo = nc.dram_tensor("bo", [128, 8], F32, kind="ExternalInput")

    C.setup_consts()
    L = LNState(C)
    xn = C.sb("xn", [128, DT * S], BF16)
    qs = C.sb("qs", [128, DT * S], BF16)
    kd = C.sb("kd", [128, 4 * S], BF16)
    vs = C.sb("vs", [128, 2 * S], BF16)
    vT = C.sb("vT", [128, 4 * 16 * 65], BF16)
    p_sl = [C.sb("psl", [128, 512], BF16) for _ in range(4)]
    rr = [C.sb("rr", [1, 512], F32) for _ in range(2)]
    rbc0 = C.sb("rbc", [64, 512], BF16)
    rbc = [rbc0, rbc0]
    wwbig = C.sb("wwbig", [128, DT * DT * 128], BF16)
    ww = [wwbig[:, 0:DT * 4 * 128], wwbig[:, DT * 4 * 128:DT * 8 * 128]]
    wwo = [wwbig[:, :]]
    qs32 = qs[:, :].bitcast(F32)
    y = qs32[:, 0:DT * TC]
    yb = L.xcb
    xc = qs32[:, DT * TC:2 * DT * TC]
    bq_s = C.sb("bq_s", [128, 8], F32)
    bk_s = C.sb("bk_s", [128, 2], F32)
    bv_s = C.sb("bv_s", [128, 2], F32)
    bo_s = C.sb("bo_s", [128, 8], F32)

    for dram, sb_ in ((bq, bq_s), (bk, bk_s), (bv, bv_s), (bo, bo_s)):
        p.step("sync", lambda E, d=dram, s=sb_: E.dma_start(
            out=s[:, :], in_=d.ap()), inc=C.s_x, inc_k=16)
    bias_ready = [(C.s_x, C.s_x.n)]
    p.wait("scalar", bias_ready)

    prev_war = []
    for b in range(BLOC):
        tok0 = b * S
        emit_pre_ln(C, L, x_in, xn, tok0=tok0, Tlen=S, xn_Ttot=S,
                    rms=False, dram_ready=C.const_ready)
        p.wait("tensor", [(C.s_ve, C.s_ve.n)])

        def ev_q(E, ps, nti, t0):
            return E.activation(
                qs[:, nti * S + t0: nti * S + t0 + TC], ps[:, :],
                AF.Identity, bias=bq_s[:, nti:nti + 1])
        linear_layer(C, x_sb=xn, KT=DT, W=wq, n_tiles=DT, evac=ev_q,
                     x_Ttot=S, slab_nt=4, wbufs=ww)

        def ev_k(E, ps, nti, t0):
            g0, g1 = 2 * nti, 2 * nti + 1
            E.activation(kd[0:64, g0 * S + t0: g0 * S + t0 + TC],
                         ps[0:64, :], AF.Identity,
                         bias=bk_s[0:64, nti:nti + 1])
            E.activation(kd[64:128, g0 * S + t0: g0 * S + t0 + TC],
                         ps[0:64, :], AF.Identity,
                         bias=bk_s[0:64, nti:nti + 1])
            E.activation(kd[0:64, g1 * S + t0: g1 * S + t0 + TC],
                         ps[64:128, :], AF.Identity,
                         bias=bk_s[64:128, nti:nti + 1])
            return E.activation(kd[64:128, g1 * S + t0: g1 * S + t0 + TC],
                                ps[64:128, :], AF.Identity,
                                bias=bk_s[64:128, nti:nti + 1])
        linear_layer(C, x_sb=xn, KT=DT, W=wk,
                     n_tiles=[(0, 128), (128, 128)], evac=ev_k,
                     x_Ttot=S, slab_nt=2, wbufs=ww)

        def ev_v(E, ps, nti, t0):
            return E.activation(
                vs[:, nti * S + t0: nti * S + t0 + TC], ps[:, :],
                AF.Identity, bias=bv_s[:, nti:nti + 1])
        linear_layer(C, x_sb=xn, KT=DT, W=wv,
                     n_tiles=[(0, 128), (128, 128)], evac=ev_v,
                     x_Ttot=S, slab_nt=2, wbufs=ww)
        kv_done = [(C.s_sc, C.s_sc.n)]
        p.step("vector", lambda E: E.memset(vT[:, :], 1.0),
               waits=[(C.s_mm, C.s_mm.n)], inc=C.s_ve)
        od = C.s_ve.n
        for g in range(4):
            emit_vT_build(C, vs, ((g % 2) * 64, (g % 2) * 64 + 64), vT,
                          g * 16 * 65, 16, src_col0=(g // 2) * S,
                          fence=kv_done + [(C.s_ve, od)])
        p.wait("tensor", [(C.s_sc, C.s_sc.n), (C.s_ve, C.s_ve.n)])

        def ctx_w(E, h, hp, tcj, av_ps, rb):
            return E.tensor_mul(
                xn[h * 64:(h + 1) * 64,
                   hp * S + tcj * TC: hp * S + tcj * TC + TC],
                av_ps, rb)
        emit_attention_b(C, q_sb=qs, q_Ttot=S, kd_sb=kd, kd_base=0,
                         vT_sb=vT, n_hp=DT, exp_scale=0.125, p_sl=p_sl,
                         rr=rr, rbc=rbc, ctx_writer=ctx_w,
                         kv_of_hp=lambda hp: ((hp // 2) * S,
                                              (hp // 2) * 16 * 65),
                         vt_h_stride=0)
        p.wait("scalar", [(C.s_ve, C.s_ve.n)])
        p.wait("tensor", [(C.s_ve, C.s_ve.n)])
        p.wait("sync", [(C.s_ve, C.s_ve.n), (C.s_mm, C.s_mm.n)])
        state = {"war": list(prev_war)}

        def ev_o(E, ps, nti, t0):
            return E.activation(
                y[:, nti * TC: nti * TC + TC], ps[:, :], AF.Identity,
                bias=bo_s[:, nti:nti + 1])

        def hook(tc, t0local, state=state, tok0=tok0):
            yr = [(C.s_sc, C.s_sc.n)]
            stc, od = emit_post_ln_res(
                C, L, y, yb, x_in, x_out, t0=tok0 + t0local,
                y_ready=yr + state["war"], xc=xc)
            state["war"] = [(C.s_ve, od)]
            p.wait("scalar", [(C.s_ve, od)])
            p.wait("sync", [(C.s_ve, od)])
            p.wait("gpsimd", [(C.s_ve, od)])
        linear_layer(C, x_sb=xn, KT=DT, W=wo, n_tiles=DT, evac=ev_o,
                     x_Ttot=S, slab_nt=DT, wbufs=wwo, tc_outer=True,
                     chunk_hook=hook)
        prev_war = state["war"]
    C.join()


def prep_l3_attn(inp, xT_list):
    def cols2(v):
        v = np.asarray(v, np.float32)
        return np.ascontiguousarray(v.reshape(2, 128).T)
    shared = {
        "wq": np.ascontiguousarray(np.asarray(inp["L3_attn_Wq"], np.float32)),
        "bq": to_cols(inp["L3_attn_bq"]),
        "wk": np.ascontiguousarray(np.asarray(inp["L3_attn_Wk"], np.float32)),
        "bk": cols2(inp["L3_attn_bk"]),
        "wv": np.ascontiguousarray(np.asarray(inp["L3_attn_Wv"], np.float32)),
        "bv": cols2(inp["L3_attn_bv"]),
        "wo": np.ascontiguousarray(
            np.asarray(inp["L3_attn_Wo"], np.float32)
            .reshape(16, 64, D)[[(h % 4) * 4 + h // 4 for h in range(16)]]
            .reshape(D, D)),
        "bo": to_cols(inp["L3_attn_bo"]),
        "c_identity": np.eye(128, dtype=np.float32),
    }
    return [dict(shared, x_in=xT_list[c]) for c in range(NCORES)]



def emit_transpose_TN(C, src, src_ofs, src_Ttot, dst, dst_ofs, dst_stride,
                      n_tt, n_kt, fence=()):
    p = C.p
    for tt in range(n_tt):
        for kt in range(n_kt):
            w = C.bank_acquire(7)
            p.step("tensor",
                   lambda E, tt=tt, kt=kt: E.transpose(
                       C.TP[:, 0:128],
                       src[:, src_ofs + kt * src_Ttot + tt * 128:
                           src_ofs + kt * src_Ttot + tt * 128 + 128],
                       C.identity[:, :]),
                   waits=list(fence) + w, inc=C.s_mm)
            td = C.s_mm.n
            p.step("vector",
                   lambda E, tt=tt, kt=kt: E.tensor_copy(
                       dst[:, dst_ofs + tt * dst_stride + kt * 128:
                           dst_ofs + tt * dst_stride + kt * 128 + 128],
                       C.TP[:, 0:128]),
                   waits=[(C.s_mm, td)], inc=C.s_ve)
            C.note_bank(7, C.s_ve)


def build_l1_ffn(C):
    nc = C.nc
    p = C.p
    x_in = nc.dram_tensor("x_in", [128, DT * T], F32, kind="ExternalInput")
    x_out = nc.dram_tensor("x_out", [128, DT * T], F32,
                           kind="ExternalOutput")
    w1 = nc.dram_tensor("w1", [D, 4096], F32, kind="ExternalInput")
    w2 = nc.dram_tensor("w2", [4096, D], F32, kind="ExternalInput")
    b1 = nc.dram_tensor("b1", [128, 32], F32, kind="ExternalInput")
    b2 = nc.dram_tensor("b2", [128, 8], F32, kind="ExternalInput")
    tern1 = nc.dram_tensor("tern1", [D, 4096], BF16)
    tern2 = nc.dram_tensor("tern2", [4096, D], BF16)

    C.setup_consts()
    L = LNState(C)
    TB = 512
    HT = 32
    xn_c = C.sb("xn_c", [128, DT * TB], BF16)
    xnN = C.sb("xnN", [128, 4 * D], BF16)
    g_T = C.sb("g_T", [128, HT * TB], BF16)
    gN = C.sb("gN", [128, 4 * 4096], BF16)
    ww = [C.sb("ww", [128, HT * 2 * 128], BF16) for _ in range(2)]
    y = C.sb("y", [128, DT * TC], F32)
    yb = L.xcb
    xc = C.sb("xc", [128, DT * TC], F32)
    b1_s = C.sb("b1_s", [128, 32], F32)
    b2_s = C.sb("b2_s", [128, 8], F32)
    ones_cf = C.sb("ones_cf", [128, 1], F32)
    wsc = C.sb("wsc", [128, 4], F32)
    rowq = C.sb("rowq", [1, 8], F32)
    xsc = C.sb("xsc", [128, 16], F32)
    xqf = C.sb("xqf", [128, 4096], F32)
    wpass = g_T[:, :].bitcast(F32)[:, 0:4096]

    for dram, sb_ in ((b1, b1_s), (b2, b2_s)):
        p.step("sync", lambda E, d=dram, s=sb_: E.dma_start(
            out=s[:, :], in_=d.ap()), inc=C.s_x, inc_k=16)
    p.step("vector", lambda E: E.memset(ones_cf[:, :], 1.0), inc=C.s_ve)
    p.wait("scalar", [(C.s_x, C.s_x.n), (C.s_ve, C.s_ve.n)])
    p.wait("tensor", [(C.s_ve, C.s_ve.n)])

    for wi, (W, TRN, KT_, NT_) in enumerate(
            ((w1, tern1, DT, HT), (w2, tern2, HT, DT))):
        w_re = W.ap().rearrange("(kt p) n -> p kt n", p=128)
        t_re = TRN.ap().rearrange("(kt p) n -> p kt n", p=128)
        ncols = NT_ * 128
        tot = KT_ * ncols
        nslab = tot // 4096
        percol = 4096 // KT_
        acc_done = None
        for sl in range(nslab):
            c0 = sl * percol
            p.step("sync", lambda E, c0=c0, percol=percol, w_re=w_re:
                   E.dma_start(out=wpass[:, :],
                               in_=w_re[:, :, c0:c0 + percol]),
                   waits=[(C.s_ve, C.s_ve.n)], inc=C.s_x, inc_k=16)
            ld = C.s_x.n
            p.wait("vector", [(C.s_x, ld)])
            if sl == 0:
                p.step("vector", lambda E: E.tensor_reduce(
                    xsc[:, 8:9], wpass[:, :],
                    AX.X, ALU.add, apply_absolute_value=True), inc=C.s_ve)
                p.wait("vector", [(C.s_ve, C.s_ve.n)])
            else:
                p.step("vector", lambda E: E.tensor_reduce(
                    xsc[:, 9:10], wpass[:, :], AX.X, ALU.add,
                    apply_absolute_value=True), inc=C.s_ve)
                p.wait("vector", [(C.s_ve, C.s_ve.n)])
                p.step("vector", lambda E: E.tensor_add(
                    xsc[:, 8:9], xsc[:, 8:9], xsc[:, 9:10]), inc=C.s_ve)
        acc_done = C.s_ve.n
        wb = C.bank_acquire(4)
        p.step("tensor", lambda E: E.matmul(
            C.B[4][0:1, 0:1], ones_cf[:, :], xsc[:, 8:9],
            start=True, stop=True),
            waits=[(C.s_ve, acc_done)] + wb, inc=C.s_mm)
        mmd = C.s_mm.n
        p.wait("vector", [(C.s_mm, mmd)])
        p.step("vector", lambda E, tot=tot: E.tensor_scalar(
            rowq[0:1, 0:1], C.B[4][0:1, 0:1], 1.0 / (tot * 128.0), 1e-5,
            ALU.mult, ALU.max), inc=C.s_ve)
        C.note_bank(4, C.s_ve)
        rq = C.s_ve.n
        wb5 = C.bank_acquire(5)
        p.step("tensor", lambda E: E.matmul(
            C.U[0][:, 0:1], C.ones_row[0:1, :], rowq[0:1, 0:1],
            start=True, stop=True), waits=[(C.s_ve, rq)] + wb5, inc=C.s_mm)
        bcd = C.s_mm.n
        p.wait("vector", [(C.s_mm, bcd)])
        p.step("vector", lambda E, wi=wi: E.tensor_copy(
            wsc[:, 2 * wi:2 * wi + 1], C.U[0][:, 0:1]))
        p.step("vector", lambda E, wi=wi: E.reciprocal(
            wsc[:, 2 * wi + 1:2 * wi + 2], C.U[0][:, 0:1]), inc=C.s_ve)
        C.note_bank(5, C.s_ve)
        inv_done = C.s_ve.n
        for sl in range(nslab):
            c0 = sl * percol
            p.step("sync", lambda E, c0=c0, percol=percol, w_re=w_re:
                   E.dma_start(out=wpass[:, :],
                               in_=w_re[:, :, c0:c0 + percol]),
                   waits=[(C.s_st, C.s_st.n), (C.s_ve, C.s_ve.n)],
                   inc=C.s_x, inc_k=16)
            ld = C.s_x.n
            p.wait("vector", [(C.s_x, ld), (C.s_ve, inv_done)])
            p.step("vector", lambda E, wi=wi: E.tensor_scalar(
                wpass[:, :], wpass[:, :], wsc[:, 2 * wi + 1:2 * wi + 2],
                None, ALU.mult))
            p.step("vector", lambda E: E.tensor_scalar(
                wpass[:, :], wpass[:, :], MAGIC, MAGIC, ALU.add,
                ALU.subtract))
            p.step("vector", lambda E: E.tensor_scalar(
                gN[:, 0:4096], wpass[:, :], 1.0, -1.0, ALU.min, ALU.max),
                inc=C.s_ve)
            tdone = C.s_ve.n
            p.step("sync", lambda E, c0=c0, percol=percol, t_re=t_re:
                   E.dma_start(out=t_re[:, :, c0:c0 + percol],
                               in_=gN[:, 0:4096]),
                   waits=[(C.s_ve, tdone)], inc=C.s_st, inc_k=16)
    p.wait("vector", [(C.s_st, C.s_st.n)])
    p.wait("gpsimd", [(C.s_st, C.s_st.n)])
    dbg_dump(C, "wsc", wsc, [(C.s_ve, C.s_ve.n)])

    prev_war = []
    for oc in range(T // TB):
        tok0 = oc * TB
        emit_pre_ln(C, L, x_in, xn_c, tok0=tok0, Tlen=TB, xn_Ttot=TB,
                    rms=False, dram_ready=C.const_ready)
        pre_done = [(C.s_ve, C.s_ve.n)]
        if oc == 0:
            dbg_dump(C, "xnc0", xn_c, pre_done)
        emit_transpose_TN(C, xn_c, 0, TB, xnN, 0, D, 4, DT,
                          fence=pre_done)
        if oc == 0:
            dbg_dump(C, "xnN", xnN, [(C.s_ve, C.s_ve.n)])
        p.wait("vector", [(C.s_ve, C.s_ve.n)])
        for tt in range(4):
            p.step("vector", lambda E, tt=tt: E.tensor_reduce(
                xsc[:, tt:tt + 1], xnN[:, tt * D:(tt + 1) * D],
                AX.X, ALU.max, apply_absolute_value=True), inc=C.s_ve)
        p.wait("vector", [(C.s_ve, C.s_ve.n)])
        p.step("vector", lambda E: E.tensor_scalar_max(
            xsc[:, 0:4], xsc[:, 0:4], 1e-5), inc=C.s_ve)
        p.wait("vector", [(C.s_ve, C.s_ve.n)])
        p.step("vector", lambda E: E.reciprocal(xsc[:, 4:8], xsc[:, 0:4]),
               inc=C.s_ve)
        p.wait("vector", [(C.s_ve, C.s_ve.n)])
        p.step("vector", lambda E: E.tensor_scalar_mul(
            xsc[:, 4:8], xsc[:, 4:8], 127.0))
        p.step("vector", lambda E: E.tensor_scalar_mul(
            xsc[:, 8:12], xsc[:, 0:4], 1.0 / 127.0), inc=C.s_ve)
        p.wait("vector", [(C.s_ve, C.s_ve.n)])
        for tt in range(4):
            p.step("vector", lambda E, tt=tt: E.tensor_scalar(
                xqf[:, 0:D], xnN[:, tt * D:(tt + 1) * D],
                xsc[:, 4 + tt:5 + tt], None, ALU.mult))
            p.step("vector", lambda E: E.tensor_scalar(
                xqf[:, 0:D], xqf[:, 0:D], MAGIC, MAGIC, ALU.add,
                ALU.subtract))
            p.step("vector", lambda E: E.tensor_scalar(
                xqf[:, 0:D], xqf[:, 0:D], 127.0, -128.0, ALU.min, ALU.max))
            p.step("vector", lambda E, tt=tt: E.tensor_scalar(
                xnN[:, tt * D:(tt + 1) * D], xqf[:, 0:D],
                xsc[:, 8 + tt:9 + tt], None, ALU.mult), inc=C.s_ve)
        if oc == 0:
            dbg_dump(C, "xqN", xnN, [(C.s_ve, C.s_ve.n)])
            dbg_dump(C, "xsc", xsc, [(C.s_ve, C.s_ve.n)])
        emit_transpose_TN(C, xnN, 0, D, xn_c, 0, TB, DT, 4,
                          fence=[(C.s_ve, C.s_ve.n)])
        p.wait("vector", [(C.s_mm, C.s_mm.n)])
        if oc == 0:
            dbg_dump(C, "xq", xn_c, [(C.s_ve, C.s_ve.n)])
        p.wait("tensor", [(C.s_ve, C.s_ve.n)])

        def ev_g(E, ps, nti, t0):
            return E.activation(
                g_T[:, nti * TB + t0: nti * TB + t0 + TC], ps[:, :],
                AF.Gelu, bias=b1_s[:, nti:nti + 1],
                scale=wsc[:, 0:1])
        linear_layer(C, x_sb=xn_c, KT=DT, W=tern1, n_tiles=HT, evac=ev_g,
                     x_Ttot=TB, slab_nt=2, wbufs=ww)
        p.wait("vector", [(C.s_sc, C.s_sc.n)])
        if oc == 0:
            dbg_dump(C, "g", g_T, [(C.s_sc, C.s_sc.n)])
        emit_transpose_TN(C, g_T, 0, TB, gN, 0, 4096, 4, HT,
                          fence=[(C.s_sc, C.s_sc.n)])
        p.wait("vector", [(C.s_ve, C.s_ve.n)])
        for tt in range(4):
            p.step("vector", lambda E, tt=tt: E.tensor_reduce(
                xsc[:, tt:tt + 1], gN[:, tt * 4096:(tt + 1) * 4096],
                AX.X, ALU.max, apply_absolute_value=True), inc=C.s_ve)
        p.wait("vector", [(C.s_ve, C.s_ve.n)])
        p.step("vector", lambda E: E.tensor_scalar_max(
            xsc[:, 0:4], xsc[:, 0:4], 1e-5), inc=C.s_ve)
        p.wait("vector", [(C.s_ve, C.s_ve.n)])
        p.step("vector", lambda E: E.reciprocal(xsc[:, 4:8], xsc[:, 0:4]),
               inc=C.s_ve)
        p.wait("vector", [(C.s_ve, C.s_ve.n)])
        p.step("vector", lambda E: E.tensor_scalar_mul(
            xsc[:, 4:8], xsc[:, 4:8], 127.0))
        p.step("vector", lambda E: E.tensor_scalar_mul(
            xsc[:, 8:12], xsc[:, 0:4], 1.0 / 127.0), inc=C.s_ve)
        p.wait("vector", [(C.s_ve, C.s_ve.n)])
        for tt in range(4):
            p.step("vector", lambda E, tt=tt: E.tensor_scalar(
                xqf[:, :], gN[:, tt * 4096:(tt + 1) * 4096],
                xsc[:, 4 + tt:5 + tt], None, ALU.mult))
            p.step("vector", lambda E: E.tensor_scalar(
                xqf[:, :], xqf[:, :], MAGIC, MAGIC, ALU.add, ALU.subtract))
            p.step("vector", lambda E: E.tensor_scalar(
                xqf[:, :], xqf[:, :], 127.0, -128.0, ALU.min, ALU.max))
            p.step("vector", lambda E, tt=tt: E.tensor_scalar(
                gN[:, tt * 4096:(tt + 1) * 4096], xqf[:, :],
                xsc[:, 8 + tt:9 + tt], None, ALU.mult), inc=C.s_ve)
        emit_transpose_TN(C, gN, 0, 4096, g_T, 0, TB, HT, 4,
                          fence=[(C.s_ve, C.s_ve.n)])
        p.wait("tensor", [(C.s_ve, C.s_ve.n)])

        def ev_y(E, ps, nti, t0):
            return E.activation(
                y[:, nti * TC: nti * TC + TC], ps[:, :],
                AF.Identity, bias=b2_s[:, nti:nti + 1],
                scale=wsc[:, 2:3])
        linear_layer(C, x_sb=g_T, KT=HT, W=tern2, n_tiles=DT, evac=ev_y,
                     x_Ttot=TB, slab_nt=2, wbufs=ww)
        yr = [(C.s_sc, C.s_sc.n)] + prev_war
        stc, od = emit_post_ln_res(C, L, y, yb, x_in, x_out, t0=tok0,
                                   y_ready=yr, xc=xc)
        prev_war = [(C.s_ve, od)]
        p.wait("scalar", [(C.s_ve, od)])
        p.wait("sync", [(C.s_ve, od)])
        p.wait("gpsimd", [(C.s_ve, od)])
    C.join()


def prep_l1_ffn(inp, xT_list):
    shared = {
        "w1": np.ascontiguousarray(np.asarray(inp["L1_ffn_W1"],
                                              np.float32).T),
        "w2": np.ascontiguousarray(np.asarray(inp["L1_ffn_W2"],
                                              np.float32).T),
        "b1": to_cols(inp["L1_ffn_b1"]),
        "b2": to_cols(inp["L1_ffn_b2"]),
        "c_identity": np.eye(128, dtype=np.float32),
    }
    return [dict(shared, x_in=xT_list[c]) for c in range(NCORES)]



def emit_rope(C, dst_sb, base, Tlen, cc, ss, prot, tmp1, tmp2, fence=()):
    p = C.p
    first = True
    for tc0 in range(0, Tlen, TC):
        pos0 = tc0 % S
        sl = slice(base + tc0, base + tc0 + TC)
        w4 = C.bank_acquire(4)
        p.step("tensor", lambda E, sl=sl: E.matmul(
            C.B[4][:, :], prot[:, :], dst_sb[:, sl], start=True, stop=True),
            waits=list(fence) + w4, inc=C.s_mm)
        first = False
        md = C.s_mm.n
        p.wait("vector", [(C.s_mm, md)])
        p.step("vector", lambda E, sl=sl, pos0=pos0: E.tensor_mul(
            tmp1[:, 0:TC], dst_sb[:, sl], cc[:, pos0:pos0 + TC]))
        p.step("vector", lambda E, pos0=pos0: E.tensor_mul(
            tmp2[:, 0:TC], C.B[4][:, :], ss[:, pos0:pos0 + TC]))
        p.step("vector", lambda E, sl=sl: E.tensor_add(
            dst_sb[:, sl], tmp1[:, 0:TC], tmp2[:, 0:TC]), inc=C.s_ve)
        C.note_bank(4, C.s_ve)


def build_l3_ffn(C):
    nc = C.nc
    p = C.p
    x_in = nc.dram_tensor("x_in", [128, DT * T], F32, kind="ExternalInput")
    x_out = nc.dram_tensor("x_out", [128, DT * T], F32,
                           kind="ExternalOutput")
    wq = nc.dram_tensor("wq", [D, D], F32, kind="ExternalInput")
    wk = nc.dram_tensor("wk", [D, D], F32, kind="ExternalInput")
    wv = nc.dram_tensor("wv", [D, D], F32, kind="ExternalInput")
    wo = nc.dram_tensor("wo", [D, D], F32, kind="ExternalInput")
    sc1 = nc.dram_tensor("sc1", [128, 8], F32, kind="ExternalInput")
    sc2 = nc.dram_tensor("sc2", [128, 8], F32, kind="ExternalInput")
    router = nc.dram_tensor("router", [128, 64], F32, kind="ExternalInput")
    we1 = nc.dram_tensor("we1", [8 * D, D], F32, kind="ExternalInput")
    we2 = nc.dram_tensor("we2", [8 * D, D], F32, kind="ExternalInput")
    weo = nc.dram_tensor("weo", [8 * D, D], F32, kind="ExternalInput")
    be1 = nc.dram_tensor("be1", [128, 64], F32, kind="ExternalInput")
    be2 = nc.dram_tensor("be2", [128, 64], F32, kind="ExternalInput")
    beo = nc.dram_tensor("beo", [8, D], F32, kind="ExternalInput")
    ccd = nc.dram_tensor("c_cc", [128, S], F32, kind="ExternalInput")
    ssd = nc.dram_tensor("c_ss", [128, S], F32, kind="ExternalInput")
    prd = nc.dram_tensor("c_prot", [128, 128], F32, kind="ExternalInput")
    mkd = [nc.dram_tensor(f"c_mask{i}", [128, 512], F32,
                          kind="ExternalInput") for i in range(4)]
    nzd = nc.dram_tensor("c_noise", [128, 32 * 8], F32, kind="ExternalInput")
    seld = nc.dram_tensor("c_sel", [8, 8 * 128], F32, kind="ExternalInput")
    xn_dram = nc.dram_tensor("xn_dram", [128, DT * T], BF16)
    ctx_dram = nc.dram_tensor("ctx_dram", [128, DT * S], BF16)
    h_dram = nc.dram_tensor("h_dram", [128, DT * T], F32)

    C.setup_consts()
    L = LNState(C)
    sc1_s = C.sb("sc1s", [128, 8], F32)
    sc2_s = C.sb("sc2s", [128, 8], F32)
    rout_s = C.sb("routs", [128, 64], BF16)
    be1_s = C.sb("be1s", [128, 64], F32)
    be2_s = C.sb("be2s", [128, 64], F32)
    beo_s = C.sb("beos", [8, D], BF16)
    cc = C.sb("cc", [128, S], BF16)
    ss = C.sb("ss", [128, S], BF16)
    prot = C.sb("prot", [128, 128], BF16)
    masks = {}
    nzN = C.sb("nzN", [128, 256], F32)
    sel_s = C.sb("sel_s", [8, 8 * 128], BF16)
    ones_bf = C.sb("onesbf", [1, 128], BF16)
    for dram, sb_ in ((sc1, sc1_s), (sc2, sc2_s), (be1, be1_s),
                     (be2, be2_s), (nzd, nzN)):
        p.step("sync", lambda E, d=dram, s=sb_: E.dma_start(
            out=s[:, :], in_=d.ap()), inc=C.s_x, inc_k=16)
    for dram, sb_ in ((router, rout_s), (beo, beo_s), (ccd, cc),
                      (ssd, ss), (prd, prot), (seld, sel_s)):
        p.step("gpsimd", lambda E, d=dram, s=sb_: E.dma_start(
            out=s[:, :], in_=d.ap()), inc=C.s_w, inc_k=16)
    for i in range(4):
        m = C.sb(f"mask{i}", [128, 512], BF16)
        masks[i * 128] = m
        p.step("gpsimd", lambda E, d=mkd[i], s=m: E.dma_start(
            out=s[:, :], in_=d.ap()), inc=C.s_w, inc_k=16)
    p.step("vector", lambda E: E.memset(ones_bf[:, :], 1.0), inc=C.s_ve)
    ready = [(C.s_x, C.s_x.n), (C.s_w, C.s_w.n), (C.s_ve, C.s_ve.n)]
    p.wait("tensor", ready)
    p.wait("scalar", ready)
    p.wait("vector", ready)

    xn_re = xn_dram.ap().rearrange("p (kt t) -> p kt t", kt=DT)
    with C.scope() as es0:
        xnc = C.sbs(es0, "xnc", [128, DT * TC], BF16)
        for i in range(T // TC):
            t0 = i * TC
            emit_pre_ln(C, L, x_in, xnc, tok0=t0, Tlen=TC, xn_Ttot=TC,
                        rms=False, dram_ready=ready
                        if i == 0 else [(C.s_st, C.s_st.n)])
            dn = C.s_ve.n
            p.step("sync", lambda E, t0=t0: E.dma_start(
                out=xn_re[:, :, t0:t0 + TC], in_=xnc[:, :]),
                waits=[(C.s_ve, dn)], inc=C.s_st, inc_k=16)
    C.join()

    prev_war = []
    for b in range(BLOC):
        tok0 = b * S
        with C.scope() as esA:
            xr = C.sbs(esA, "xr", [128, DT * S], BF16)
            ks = C.sbs(esA, "ks", [128, DT * S], BF16)
            vs = C.sbs(esA, "vs", [128, DT * S], BF16)
            qh = C.sbs(esA, "qh", [128, S], BF16)
            vT = C.sbs(esA, "vT", [128, 2 * 16 * 65], BF16)
            p_sl = [C.sbs(esA, "psl", [128, 512], BF16) for _ in range(4)]
            tmp1 = C.sbs(esA, "tmp1", [128, 512], BF16)
            tmp2 = C.sbs(esA, "tmp2", [128, 512], BF16)
            rr = [C.sbs(esA, "rr", [1, 512], F32) for _ in range(2)]
            rbc0 = C.sbs(esA, "rbc", [64, 512], BF16)
            rbc = [rbc0, rbc0]
            ww = [C.sbs(esA, "wwA", [128, DT * 2 * 128], BF16)
                  for _ in range(2)]
            yat = C.sbs(esA, "yat", [128, DT * TC], F32)
            ctc = C.sbs(esA, "ctc", [128, DT * TC], BF16)
            wwo2 = C.sbs(esA, "wwo2", [128, DT * 4 * 128], BF16)
            emit_pre_ln(C, L, xn_dram, xr, tok0=tok0, Tlen=S, xn_Ttot=S,
                        rms=True, eps=1e-6, nw_sb=sc1_s,
                        dram_ready=[(C.s_st, C.s_st.n)])
            p.wait("tensor", [(C.s_ve, C.s_ve.n)])

            def ev_k(E, ps, nti, t0):
                return E.activation(
                    ks[:, nti * S + t0: nti * S + t0 + TC], ps[:, :],
                    AF.Identity)
            linear_layer(C, x_sb=xr, KT=DT, W=wk, n_tiles=DT, evac=ev_k,
                         x_Ttot=S, slab_nt=2, wbufs=ww)
            kf = [(C.s_sc, C.s_sc.n)]
            for nt in range(DT):
                emit_rope(C, ks, nt * S, S, cc, ss, prot, tmp1, tmp2,
                          fence=kf)

            def ev_v(E, ps, nti, t0):
                return E.activation(
                    vs[:, nti * S + t0: nti * S + t0 + TC], ps[:, :],
                    AF.Identity)
            linear_layer(C, x_sb=xr, KT=DT, W=wv, n_tiles=DT, evac=ev_v,
                         x_Ttot=S, slab_nt=2, wbufs=ww)
            kv_done = [(C.s_sc, C.s_sc.n), (C.s_ve, C.s_ve.n)]

            for hp in range(DT):
                def ev_q(E, ps, nti, t0):
                    return E.activation(qh[:, t0:t0 + TC], ps[:, :],
                                        AF.Identity)
                linear_layer(C, x_sb=xr, KT=DT,
                             W=wq.ap()[:, hp * 128:(hp + 1) * 128],
                             n_tiles=1, evac=ev_q, x_Ttot=S, slab_nt=1,
                             wbufs=ww)
                emit_rope(C, qh, 0, S, cc, ss, prot, tmp1, tmp2,
                          fence=[(C.s_sc, C.s_sc.n)])
                p.wait("vector", [(C.s_mm, C.s_mm.n)])
                p.step("vector", lambda E: E.memset(vT[:, :], 1.0),
                       waits=kv_done, inc=C.s_ve)
                for h in range(2):
                    emit_vT_build(C, vs, (h * 64, (h + 1) * 64), vT,
                                  h * 16 * 65, 16, src_col0=hp * S,
                                  fence=[(C.s_ve, C.s_ve.n)])
                p.wait("tensor", [(C.s_ve, C.s_ve.n)])

                def ctx_w(E, h, hp_, tcj, av_ps, rb):
                    return E.tensor_mul(
                        qh[h * 64:(h + 1) * 64, tcj * TC: tcj * TC + TC],
                        av_ps, rb)
                emit_attention_b(C, q_sb=qh, q_Ttot=S, kd_sb=ks,
                                 kd_base=hp * S, vT_sb=vT, n_hp=1,
                                 exp_scale=0.125, causal_t0=0,
                                 p_sl=p_sl, rr=rr, rbc=rbc,
                                 ctx_writer=ctx_w, masks=masks,
                                 vt_h_stride=16 * 65)
                cd = C.s_ve.n
                p.step("sync", lambda E, hp=hp: E.dma_start(
                    out=ctx_dram.ap()[:, hp * S:(hp + 1) * S],
                    in_=qh[:, :]), waits=[(C.s_ve, cd)],
                    inc=C.s_st, inc_k=16)
                p.wait("tensor", [(C.s_st, C.s_st.n)])
                p.wait("scalar", [(C.s_st, C.s_st.n)])
            ct_re = ctx_dram.ap().rearrange("p (kt t) -> p kt t", kt=DT)
            h_re = h_dram.ap().rearrange("p (kt t) -> p kt t", kt=DT)
            wo_re = wo.ap().rearrange("(kt p) n -> p kt n", p=128)
            p.step("gpsimd", lambda E: E.dma_start(
                out=ww[0][:, :], in_=wo_re[:, :, 0:256]),
                waits=[(C.s_wf, C.s_wf.n)], inc=C.s_wl[0], inc_k=16)
            p.step("gpsimd", lambda E: E.dma_start(
                out=ww[1][:, :], in_=wo_re[:, :, 256:512]),
                inc=C.s_wl[1], inc_k=16)
            p.step("gpsimd", lambda E: E.dma_start(
                out=wwo2[:, 0:2048],
                in_=wo_re[:, :, 512:768]), inc=C.s_wl[2], inc_k=16)
            p.step("gpsimd", lambda E: E.dma_start(
                out=wwo2[:, 2048:4096],
                in_=wo_re[:, :, 768:1024]), inc=C.s_wl[3], inc_k=16)
            wol_w = [(s, s.n) for s in C.s_wl]
            wosl = [ww[0], ww[1], wwo2[:, 0:2048], wwo2[:, 2048:4096]]
            for tc in range(S // TC):
                t0 = tok0 + tc * TC
                lofs = t0 - tok0
                p.step("sync", lambda E, lofs=lofs: E.dma_start(
                    out=ctc[:, :],
                    in_=ct_re[:, :, lofs:lofs + TC]),
                    waits=[(C.s_ve, C.s_ve.n), (C.s_st, C.s_st.n)],
                    inc=C.s_x, inc_k=16)
                cl = C.s_x.n
                p.wait("tensor", [(C.s_x, cl)] + wol_w)
                for nt in range(DT):
                    bi = nt % 4
                    ps = C.B[bi]
                    w0 = C.bank_acquire(bi)
                    for kt in range(DT):
                        def fmm(E, nt=nt, kt=kt, ps=ps):
                            buf = wosl[nt // 2]
                            ofs = (nt % 2) * 128
                            return E.matmul(
                                ps[:, :],
                                buf[:, kt * 256 + ofs: kt * 256 + ofs + 128],
                                ctc[:, kt * TC:(kt + 1) * TC],
                                start=(kt == 0), stop=(kt == DT - 1))
                        p.step("tensor", fmm, waits=w0 if kt == 0 else (),
                               inc=C.s_mm if kt == DT - 1 else None)
                    md = C.s_mm.n
                    p.step("scalar", lambda E, ps=ps, nt=nt: E.activation(
                        yat[:, nt * TC:(nt + 1) * TC], ps[:, :],
                        AF.Identity), waits=[(C.s_mm, md)], inc=C.s_sc)
                    C.note_bank(bi, C.s_sc)
                p.step("gpsimd", lambda E, t0=t0: E.dma_start(
                    out=L.xcb[:, :], in_=xn_re[:, :, t0:t0 + TC]),
                    waits=[(C.s_ve, C.s_ve.n)], inc=C.s_w, inc_k=16)
                xl = C.s_w.n
                p.wait("vector", [(C.s_sc, C.s_sc.n), (C.s_w, xl)])
                for kt in range(DT):
                    p.step("vector", lambda E, kt=kt: E.tensor_add(
                        yat[:, kt * TC:(kt + 1) * TC],
                        yat[:, kt * TC:(kt + 1) * TC],
                        L.xcb[:, kt * TC:(kt + 1) * TC]),
                        inc=C.s_ve if kt == DT - 1 else None)
                hd = C.s_ve.n
                p.step("sync", lambda E, t0=t0: E.dma_start(
                    out=h_re[:, :, t0:t0 + TC], in_=yat[:, :]),
                    waits=[(C.s_ve, hd)], inc=C.s_st, inc_k=16)
                p.wait("scalar", [(C.s_ve, hd)])
                p.wait("gpsimd", [(C.s_ve, hd)])
        C.join()
        with C.scope() as esB:
            hrc = C.sbs(esB, "hrc", [128, DT * TC], BF16)
            hfc = C.sbs(esB, "hfc", [128, DT * TC], F32)
            aec = C.sbs(esB, "aec", [128, DT * TC], BF16)
            macc = C.sbs(esB, "macc", [128, DT * TC], F32)
            xc = C.sbs(esB, "xc", [128, DT * TC], F32)
            lgN = C.sbs(esB, "lgN", [128, 32], F32)
            prN = C.sbs(esB, "prN", [128, 32], F32)
            wNt = C.sbs(esB, "wNt", [128, 32], BF16)
            sb8 = C.sbs(esB, "sb8", [8, 512], BF16)
            lgt = C.sbs(esB, "lgt", [8, 512], BF16)
            rsc = C.sbs(esB, "rsc", [128, 16], F32)
            wwB = [C.sbs(esB, "wwB", [128, DT * 2 * 128], BF16)
                   for _ in range(2)]
            for oc in range(S // TC):
                t0 = tok0 + oc * TC
                t0l = oc * TC
                p.step("sync", lambda E, t0=t0: E.dma_start(
                    out=hfc[:, :], in_=h_re[:, :, t0:t0 + TC]),
                    waits=[(C.s_st, C.s_st.n), (C.s_ve, C.s_ve.n)],
                    inc=C.s_x, inc_k=16)
                hl = C.s_x.n
                p.wait("vector", [(C.s_x, hl)])
                p.step("vector", lambda E: E.tensor_copy(
                    L.xcb[:, :], hfc[:, :]), inc=C.s_ve)
                hbd = C.s_ve.n
                fence = ln_chunk(C, L, L.xcb, 0, TC, rms=True, eps=1e-6,
                                 fence_in=[(C.s_ve, hbd)])
                p.wait("vector", fence)
                for kt in range(DT):
                    p.step("vector", lambda E, kt=kt:
                           E.scalar_tensor_tensor(
                               hrc[:, kt * TC:(kt + 1) * TC],
                               L.xcb[:, kt * TC:(kt + 1) * TC],
                               sc2_s[:, kt:kt + 1], C.U[0][:, :],
                               ALU.mult, ALU.mult),
                           inc=C.s_ve if kt == DT - 1 else None)
                C.note_bank(5, C.s_ve)
                C.note_bank(6, C.s_ve)
                hr_done = C.s_ve.n
                p.wait("tensor", [(C.s_ve, hr_done)])
                w4 = C.bank_acquire(4)
                for kt in range(DT):
                    p.step("tensor", lambda E, kt=kt: E.matmul(
                        C.B[4][0:8, :], rout_s[:, kt * 8:(kt + 1) * 8],
                        hrc[:, kt * TC:(kt + 1) * TC],
                        start=(kt == 0), stop=(kt == DT - 1)),
                        waits=w4 if kt == 0 else (),
                        inc=C.s_mm if kt == DT - 1 else None)
                lgd = C.s_mm.n
                p.step("scalar", lambda E: E.activation(
                    lgt[:, :], C.B[4][0:8, :], AF.Identity),
                    waits=[(C.s_mm, lgd)], inc=C.s_sc)
                C.note_bank(4, C.s_sc)
                lge = C.s_sc.n
                for q4 in range(4):
                    w7 = C.bank_acquire(7)
                    p.step("tensor", lambda E, q4=q4: E.transpose(
                        C.TP[:, 0:8], lgt[0:8, q4 * 128:(q4 + 1) * 128],
                        C.identity[0:8, 0:8]),
                        waits=([(C.s_sc, lge)] + w7) if q4 == 0 else w7,
                        inc=C.s_mm)
                    td = C.s_mm.n
                    p.step("vector", lambda E, q4=q4, oc=oc: E.tensor_copy(
                        lgN[:, q4 * 8:(q4 + 1) * 8], C.TP[:, 0:8]),
                        waits=[(C.s_mm, td)], inc=C.s_ve)
                    C.note_bank(7, C.s_ve)
                tt0 = (b * 16 + oc * 4) * 8
                p.step("vector", lambda E, tt0=tt0: E.tensor_add(
                    lgN[:, 0:32], lgN[:, 0:32],
                    nzN[:, tt0:tt0 + 32]), inc=C.s_ve)
                ad = C.s_ve.n
                p.wait("scalar", [(C.s_ve, ad)])
                p.step("scalar", lambda E: E.activation(
                    prN[:, 0:32], lgN[:, 0:32], AF.Exp), inc=C.s_sc)
                p.wait("vector", [(C.s_sc, C.s_sc.n)])
                p.step("vector", lambda E: E.tensor_reduce(
                    rsc[:, 0:4], prN[:, 0:32].rearrange(
                        "p (a b) -> p a b", a=4), AX.X, ALU.add),
                    inc=C.s_ve)
                p.wait("vector", [(C.s_ve, C.s_ve.n)])
                p.step("vector", lambda E: E.reciprocal(
                    rsc[:, 4:8], rsc[:, 0:4]), inc=C.s_ve)
                p.wait("vector", [(C.s_ve, C.s_ve.n)])
                for a4 in range(4):
                    p.step("vector", lambda E, a4=a4: E.tensor_scalar(
                        prN[:, a4 * 8:(a4 + 1) * 8],
                        prN[:, a4 * 8:(a4 + 1) * 8],
                        rsc[:, 4 + a4:5 + a4], None, ALU.mult))
                p.step("vector", lambda E: E.tensor_reduce(
                    rsc[:, 8:12], prN[:, 0:32].rearrange(
                        "p (a b) -> p a b", a=4), AX.X, ALU.max),
                    inc=C.s_ve)
                p.wait("vector", [(C.s_ve, C.s_ve.n)])
                for a4 in range(4):
                    p.step("vector", lambda E, a4=a4: E.tensor_scalar(
                        wNt[:, a4 * 8:(a4 + 1) * 8],
                        prN[:, a4 * 8:(a4 + 1) * 8],
                        rsc[:, 8 + a4:9 + a4], None, ALU.is_ge))
                    p.step("vector", lambda E, a4=a4:
                           E.scalar_tensor_tensor(
                               lgN[:, a4 * 8:(a4 + 1) * 8],
                               wNt[:, a4 * 8:(a4 + 1) * 8], -2.0,
                               prN[:, a4 * 8:(a4 + 1) * 8],
                               ALU.mult, ALU.add),
                           inc=C.s_ve if a4 == 3 else None)
                p.step("vector", lambda E: E.tensor_reduce(
                    rsc[:, 12:16], lgN[:, 0:32].rearrange(
                        "p (a b) -> p a b", a=4), AX.X, ALU.max),
                    inc=C.s_ve)
                p.wait("vector", [(C.s_ve, C.s_ve.n)])
                for a4 in range(4):
                    p.step("vector", lambda E, a4=a4: E.tensor_scalar(
                        wNt[:, a4 * 8:(a4 + 1) * 8],
                        prN[:, a4 * 8:(a4 + 1) * 8],
                        rsc[:, 12 + a4:13 + a4], None, ALU.is_ge))
                    p.step("vector", lambda E, a4=a4: E.tensor_mul(
                        wNt[:, a4 * 8:(a4 + 1) * 8],
                        wNt[:, a4 * 8:(a4 + 1) * 8],
                        prN[:, a4 * 8:(a4 + 1) * 8]),
                        inc=C.s_ve if a4 == 3 else None)
                wd = C.s_ve.n
                for q4 in range(4):
                    w7 = C.bank_acquire(7)
                    p.step("tensor", lambda E, q4=q4: E.transpose(
                        C.TP[0:8, 0:128],
                        wNt[:, q4 * 8:(q4 + 1) * 8],
                        C.identity[:, :]),
                        waits=([(C.s_ve, wd)] + w7) if q4 == 0 else w7,
                        inc=C.s_mm)
                    td = C.s_mm.n
                    p.step("vector", lambda E, q4=q4: E.tensor_copy(
                        sb8[0:8, q4 * 128:(q4 + 1) * 128],
                        C.TP[0:8, 0:128]), waits=[(C.s_mm, td)],
                        inc=C.s_ve)
                    C.note_bank(7, C.s_ve)
                gate_done = [(C.s_ve, C.s_ve.n)]
                p.wait("tensor", gate_done)
                for e in range(8):
                    w5 = C.bank_acquire(5)
                    for q4 in range(4):
                        p.step("tensor", lambda E, e=e, q4=q4: E.matmul(
                            C.U[0][:, q4 * 128:(q4 + 1) * 128],
                            sel_s[0:8, e * 128:(e + 1) * 128],
                            sb8[0:8, q4 * 128:(q4 + 1) * 128],
                            start=True, stop=True),
                            waits=w5 if q4 == 0 else (), inc=C.s_mm)
                    gb = C.s_mm.n

                    def ev_a(E, ps, nti, t0_, e=e):
                        return E.activation(
                            aec[:, nti * TC: nti * TC + TC], ps[:, :],
                            AF.Silu, bias=be1_s[:, e * 8 + nti:
                                                e * 8 + nti + 1])
                    linear_layer(C, x_sb=hrc, KT=DT,
                                 W=we1.ap()[e * D:(e + 1) * D, :],
                                 n_tiles=DT, evac=ev_a, x_Ttot=TC,
                                 slab_nt=2, wbufs=wwB, banks=(0, 1),
                                 tok0=0)
                    p.wait("vector", [(C.s_sc, C.s_sc.n)])

                    def ev_g(E, ps, nti, t0_, e=e):
                        return E.scalar_tensor_tensor(
                            aec[:, nti * TC: nti * TC + TC], ps[:, :],
                            be2_s[:, e * 8 + nti: e * 8 + nti + 1],
                            aec[:, nti * TC: nti * TC + TC],
                            ALU.add, ALU.mult)
                    linear_layer(C, x_sb=hrc, KT=DT,
                                 W=we2.ap()[e * D:(e + 1) * D, :],
                                 n_tiles=DT, evac=ev_g, evac_eng="vector",
                                 x_Ttot=TC, slab_nt=2, wbufs=wwB,
                                 banks=(0, 1), tok0=0)
                    p.wait("vector", [(C.s_mm, gb)])
                    for kt in range(DT):
                        p.step("vector", lambda E, kt=kt: E.tensor_mul(
                            aec[:, kt * TC:(kt + 1) * TC],
                            aec[:, kt * TC:(kt + 1) * TC],
                            C.U[0][:, :]),
                            inc=C.s_ve if kt == DT - 1 else None)
                    C.note_bank(5, C.s_ve)
                    p.wait("tensor", [(C.s_ve, C.s_ve.n)])

                    def ev_o2(E, ps, nti, t0_, e=e):
                        if e == 0:
                            return E.tensor_copy(
                                macc[:, nti * TC:(nti + 1) * TC], ps[:, :])
                        return E.tensor_add(
                            macc[:, nti * TC:(nti + 1) * TC],
                            macc[:, nti * TC:(nti + 1) * TC], ps[:, :])
                    linear_layer(C, x_sb=aec, KT=DT,
                                 W=weo.ap()[e * D:(e + 1) * D, :],
                                 n_tiles=DT, evac=ev_o2, evac_eng="vector",
                                 x_Ttot=TC, slab_nt=2, wbufs=wwB,
                                 banks=(2, 3), tok0=0)
                for nt in range(DT):
                    bi = (2, 3)[nt % 2]
                    w0 = C.bank_acquire(bi)
                    p.step("tensor", lambda E, nt=nt, bi=bi: E.matmul(
                        C.B[bi][:, :], beo_s[0:8, nt * 128:(nt + 1) * 128],
                        sb8[0:8, :], start=True, stop=True),
                        waits=w0, inc=C.s_mm)
                    md = C.s_mm.n
                    p.step("vector", lambda E, nt=nt, bi=bi: E.tensor_add(
                        macc[:, nt * TC:(nt + 1) * TC],
                        macc[:, nt * TC:(nt + 1) * TC], C.B[bi][:, :]),
                        waits=[(C.s_mm, md)], inc=C.s_ve)
                    C.note_bank(bi, C.s_ve)
                p.wait("vector", [(C.s_ve, C.s_ve.n)])
                for kt in range(DT):
                    p.step("vector", lambda E, kt=kt: E.tensor_add(
                        macc[:, kt * TC:(kt + 1) * TC],
                        macc[:, kt * TC:(kt + 1) * TC],
                        hfc[:, kt * TC:(kt + 1) * TC]),
                        inc=C.s_ve if kt == DT - 1 else None)
                yr = [(C.s_ve, C.s_ve.n)] + prev_war
                stc, od = emit_post_ln_res(C, L, macc, L.xcb, x_in, x_out,
                                           t0=t0, y_ready=yr, xc=xc)
                prev_war = [(C.s_ve, od)]
                p.wait("scalar", [(C.s_ve, od)])
                p.wait("sync", [(C.s_ve, od)])
                p.wait("gpsimd", [(C.s_ve, od)])
                p.wait("vector", [(C.s_ve, od)])
        C.join()
    C.join()
    dbg_dump_dram(C, "xn", xn_dram)
    dbg_dump_dram(C, "ctx", ctx_dram)
    dbg_dump_dram(C, "h", h_dram)


def prep_l3_ffn(inp, xT_list):
    import jax
    H, DK = 16, 64
    half = DK // 2
    inv = 1.0 / 10000.0 ** (np.arange(half, dtype=np.float32) / half)
    ang = np.arange(S, dtype=np.float32)[:, None] * inv[None, :]
    sin = np.sin(ang).T
    cos = np.cos(ang).T
    cc = np.tile(cos, (4, 1)).astype(np.float32)
    ssm = np.concatenate([-sin, sin, -sin, sin], 0).astype(np.float32)
    pr = np.zeros((128, 128), np.float32)
    for hh in range(2):
        for i in range(32):
            pr[hh * 64 + 32 + i, hh * 64 + i] = 1.0
            pr[hh * 64 + i, hh * 64 + 32 + i] = 1.0
    masks = {}
    for i in range(4):
        diff = i * 128
        mk = np.zeros((128, 512), np.float32)
        for p_ in range(128):
            f0 = p_ + diff
            if f0 < 512:
                mk[p_, f0:] = 1.0
        masks[i] = mk
    with jax.default_device(jax.devices("cpu")[0]):
        noise = np.asarray(
            jax.random.gumbel(jax.random.key(42), (16, S, 8))) * 0.05
    b1c = np.zeros((128, 64), np.float32)
    b2c = np.zeros((128, 64), np.float32)
    for e in range(8):
        b1c[:, e * 8:(e + 1) * 8] = to_cols(
            np.asarray(inp["L3_ffn_b1"], np.float32)[e])
        b2c[:, e * 8:(e + 1) * 8] = to_cols(
            np.asarray(inp["L3_ffn_b2"], np.float32)[e])
    rout = np.asarray(inp["L3_ffn_router_w"], np.float32)
    routc = np.ascontiguousarray(
        rout.reshape(8, 128, 8).transpose(1, 0, 2).reshape(128, 64))
    shared = {
        "wq": np.ascontiguousarray(np.asarray(inp["L3_ffn_wq"], np.float32)),
        "wk": np.ascontiguousarray(np.asarray(inp["L3_ffn_wk"], np.float32)),
        "wv": np.ascontiguousarray(np.asarray(inp["L3_ffn_wv"], np.float32)),
        "wo": np.ascontiguousarray(np.asarray(inp["L3_ffn_wo"], np.float32)),
        "sc1": to_cols(inp["L3_ffn_scale1"]),
        "sc2": to_cols(inp["L3_ffn_scale2"]),
        "router": routc,
        "we1": np.asarray(inp["L3_ffn_w1"], np.float32).reshape(8 * D, D),
        "we2": np.asarray(inp["L3_ffn_w2"], np.float32).reshape(8 * D, D),
        "weo": np.asarray(inp["L3_ffn_wout"], np.float32).reshape(8 * D, D),
        "be1": b1c,
        "be2": b2c,
        "beo": np.asarray(inp["L3_ffn_bout"], np.float32),
        "c_cc": cc, "c_ss": ssm, "c_prot": pr,
        "c_sel": np.concatenate(
            [np.tile(np.eye(8, dtype=np.float32)[:, e:e + 1], (1, 128))
             for e in range(8)], axis=1),
        "c_identity": np.eye(128, dtype=np.float32),
    }
    for i in range(4):
        shared[f"c_mask{i}"] = masks[i]
    maps = []
    for c in range(NCORES):
        nz = noise[2 * c:2 * c + 2].reshape(T, 8)
        nzN = np.ascontiguousarray(
            nz.reshape(32, 128, 8).transpose(1, 0, 2).reshape(128, 256))
        maps.append(dict(shared, x_in=xT_list[c], c_noise=nzN))
    return maps



def build_l1_attn(C):
    nc = C.nc
    p = C.p
    x_in = nc.dram_tensor("x_in", [128, DT * T], F32, kind="ExternalInput")
    x_out = nc.dram_tensor("x_out", [128, DT * T], F32,
                           kind="ExternalOutput")
    wqd = nc.dram_tensor("wq", [D, D], F32, kind="ExternalInput")
    bq = nc.dram_tensor("bq", [128, 8], F32, kind="ExternalInput")
    wkd = nc.dram_tensor("wk", [D, D], F32, kind="ExternalInput")
    bk = nc.dram_tensor("bk", [128, 8], F32, kind="ExternalInput")
    wvd = nc.dram_tensor("wv", [D, D], F32, kind="ExternalInput")
    bv = nc.dram_tensor("bv", [128, 8], F32, kind="ExternalInput")
    wod = nc.dram_tensor("wo", [D, D], F32, kind="ExternalInput")
    bo = nc.dram_tensor("bo", [128, 8], F32, kind="ExternalInput")
    reld = nc.dram_tensor("rel", [32, 16], F32, kind="ExternalInput")
    gwd = nc.dram_tensor("gw", [64, 8], F32, kind="ExternalInput")
    gbd = nc.dram_tensor("gb", [1, 8], F32, kind="ExternalInput")
    gad = nc.dram_tensor("ga", [128, 1], F32, kind="ExternalInput")
    ohd = nc.dram_tensor("c_oh", [32, 256], F32, kind="ExternalInput")
    indd = nc.dram_tensor("c_ind", [1, 256], F32, kind="ExternalInput")
    NQ = 3 * D * T
    agin = nc.dram_tensor("agin", [1, NQ], BF16)
    agout = nc.dram_tensor("agout", [8, NQ], BF16)
    ggin = nc.dram_tensor("ggin", [1, 16 * T], F32)
    ggout = nc.dram_tensor("ggout", [8, 16 * T], F32)
    NC2 = 16 * D * 256
    c2in = nc.dram_tensor("c2in", [1, NC2], BF16)
    c2out = nc.dram_tensor("c2out", [8, NC2], BF16)
    stq = nc.dram_tensor("stq", [8, 3 * D * 2 * 256], BF16)
    stg2 = nc.dram_tensor("stg2", [8, 16 * 2 * 256], F32)
    stc = nc.dram_tensor("stc", [8, 2 * 256 * D], BF16)
    s_cc1 = p.sem("s_cc1")
    s_cc2 = p.sem("s_cc2")
    s_cc3 = p.sem("s_cc3")

    C.setup_consts()
    L = LNState(C)
    biases = {}
    for nm, dram in (("bq", bq), ("bk", bk), ("bv", bv), ("bo", bo)):
        sb_ = C.sb(nm, [128, 8], F32)
        biases[nm] = sb_
        p.step("sync", lambda E, d=dram, s=sb_: E.dma_start(
            out=s[:, :], in_=d.ap()), inc=C.s_x, inc_k=16)
    rel_s = C.sb("rel_s", [32, 16], BF16)
    gw_s = C.sb("gw_s", [64, 8], F32)
    gb_s = C.sb("gb_s", [1, 8], F32)
    ga_s = C.sb("ga_s", [128, 1], F32)
    oh_s = C.sb("oh_s", [32, 256], BF16)
    ind_s = C.sb("ind_s", [1, 256], F32)
    for dram, sb_ in ((reld, rel_s), (ohd, oh_s)):
        p.step("gpsimd", lambda E, d=dram, s=sb_: E.dma_start(
            out=s[:, :], in_=d.ap()), inc=C.s_w, inc_k=16)
    p.step("sync", lambda E: E.dma_start(
        out=ind_s[:, :], in_=indd.ap()), inc=C.s_x, inc_k=16)
    for dram, sb_ in ((gwd, gw_s), (gbd, gb_s), (gad, ga_s)):
        p.step("sync", lambda E, d=dram, s=sb_: E.dma_start(
            out=s[:, :], in_=d.ap()), inc=C.s_x, inc_k=16)
    bld = C.s_x.n
    p.step("vector", lambda E: E.tensor_scalar_mul(
        biases["bq"][:, :], biases["bq"][:, :], 0.125),
        waits=[(C.s_x, bld)], inc=C.s_ve)
    gw2 = C.sb("gw2", [64, 2], F32)
    gb2 = C.sb("gb2", [1, 2], F32)
    p.step("vector", lambda E: E.tensor_reduce(
        gw2[0:64, :], gw_s[0:64, :].rearrange("p (c j) -> p c j", c=2),
        AX.X, ALU.add), inc=C.s_ve)
    p.step("vector", lambda E: E.tensor_reduce(
        gb2[0:1, :], gb_s[0:1, :].rearrange("p (c j) -> p c j", c=2),
        AX.X, ALU.add), inc=C.s_ve)
    p.wait("vector", [(C.s_ve, C.s_ve.n)])
    wg_sb = C.sb("wg", [128, DT * 128], BF16)
    bvec = C.sb("bvec", [128, 1], F32)
    p.step("vector", lambda E: E.memset(wg_sb[:, :], 0.0), inc=C.s_ve)
    p.wait("vector", [(C.s_ve, C.s_ve.n)])
    for kt in range(DT):
        for c in range(2):
            for hh in range(2):
                h = 2 * kt + hh
                r0 = hh * 64
                p.step("vector", lambda E, kt=kt, c=c, h=h, r0=r0:
                       E.tensor_copy(
                           wg_sb[r0:r0 + 64,
                                 kt * 128 + c * 64 + h:
                                 kt * 128 + c * 64 + h + 1],
                           gw2[0:64, c:c + 1]))
    p.step("vector", lambda E: E.nop(), inc=C.s_ve)
    wg_done = C.s_ve.n
    wb5 = C.bank_acquire(5)
    p.step("tensor", lambda E: E.matmul(
        C.U[0][:, 0:1], ind_s[0:1, 0:128], gb2[0:1, 0:1],
        start=True, stop=False), waits=[(C.s_ve, wg_done)] + wb5)
    p.step("tensor", lambda E: E.matmul(
        C.U[0][:, 0:1], ind_s[0:1, 128:256], gb2[0:1, 1:2],
        start=False, stop=True), inc=C.s_mm)
    bvd = C.s_mm.n
    p.step("vector", lambda E: E.tensor_copy(bvec[:, :], C.U[0][:, 0:1]),
           waits=[(C.s_mm, bvd)], inc=C.s_ve)
    C.note_bank(5, C.s_ve)
    consts_done = [(C.s_ve, C.s_ve.n), (C.s_w, C.s_w.n), (C.s_x, C.s_x.n)]
    p.wait("tensor", consts_done)
    p.wait("scalar", consts_done)

    ag_re = agin.ap().rearrange("o (g f tl b) -> o g f tl b", g=3, f=D,
                                tl=2, b=S)

    with C.scope() as esP:
        xn = C.sbs(esP, "xn", [128, DT * T], BF16)
        qtmp = C.sbs(esP, "qtmp", [128, DT * TC], BF16)
        gsg = C.sbs(esP, "gsg", [128, T], F32)
        ga1 = C.sbs(esP, "ga1", [128, T], F32)
        gt1 = C.sbs(esP, "gt1", [16, T], F32)
        wwbig = C.sbs(esP, "wwP", [128, DT * DT * 128], BF16)
        emit_pre_ln(C, L, x_in, xn, tok0=0, Tlen=T, xn_Ttot=T,
                    rms=False, dram_ready=C.const_ready)
        p.wait("tensor", [(C.s_ve, C.s_ve.n)])
        for reg, (wd, bn, scl) in enumerate(
                (("wq", "bq", 0.125), ("wk", "bk", 1.0),
                 ("wv", "bv", 1.0))):
            wdr = {"wq": wqd, "wk": wkd, "wv": wvd}[wd]

            def ev(E, ps, nti, t0, bn=bn, scl=scl):
                return E.activation(
                    qtmp[:, nti * TC: nti * TC + TC], ps[:, :],
                    AF.Identity, bias=biases[bn][:, nti:nti + 1],
                    scale=scl)

            def hook(tc, t0, reg=reg):
                dn = C.s_sc.n
                tl, b0 = t0 // S, t0 % S
                p.step("sync", lambda E, reg=reg, tl=tl, b0=b0:
                       E.dma_start(
                           out=ag_re[0, reg, :, tl, b0:b0 + TC].rearrange(
                               "(nt p) b -> p nt b", p=128),
                           in_=qtmp[:, :].rearrange(
                               "p (nt t) -> p nt t", nt=DT)),
                       waits=[(C.s_sc, dn)], inc=C.s_st, inc_k=16)
                p.wait("scalar", [(C.s_st, C.s_st.n)])
            linear_layer(C, x_sb=xn, KT=DT, W=wdr, n_tiles=DT, evac=ev,
                         x_Ttot=T, slab_nt=DT, wbufs=[wwbig],
                         tc_outer=True, chunk_hook=hook)
        for tc in range(T // TC):
            t0 = tc * TC
            bi = tc % 2
            ps = C.B[bi]
            w0 = C.bank_acquire(bi)
            for kt in range(DT):
                p.step("tensor", lambda E, kt=kt, t0=t0, ps=ps: E.matmul(
                    ps[:, :], wg_sb[:, kt * 128:(kt + 1) * 128],
                    xn[:, kt * T + t0: kt * T + t0 + TC],
                    start=(kt == 0), stop=(kt == DT - 1)),
                    waits=w0 if kt == 0 else (),
                    inc=C.s_mm if kt == DT - 1 else None)
            md = C.s_mm.n
            p.step("scalar", lambda E, ps=ps, t0=t0: E.activation(
                gsg[:, t0:t0 + TC], ps[:, :], AF.Sigmoid,
                bias=bvec[:, 0:1]), waits=[(C.s_mm, md)], inc=C.s_sc)
            C.note_bank(bi, C.s_sc)
        sgd = C.s_sc.n
        p.wait("vector", [(C.s_sc, sgd)])
        p.step("vector", lambda E: E.tensor_scalar(
            gt1[0:16, :], gsg[64:80, :], ga_s[64:80, 0:1], -1.0,
            ALU.mult, ALU.add))
        p.step("vector", lambda E: E.tensor_mul(
            ga1[0:16, :], gsg[0:16, :], gt1[0:16, :]))
        p.step("vector", lambda E: E.tensor_scalar_add(
            ga1[0:16, :], ga1[0:16, :], 2.0), inc=C.s_ve)
        gad_ = C.s_ve.n
        p.step("sync", lambda E: E.dma_start(
            out=ggin.ap().rearrange("o (h tl b) -> o h tl b", h=16, b=S)
            [0, :, :, :], in_=ga1[0:16, :].rearrange(
                "p (tl b) -> p tl b", b=S)),
            waits=[(C.s_ve, gad_)], inc=C.s_st, inc_k=16)
    p.wait("gpsimd", [(C.s_st, C.s_st.n)])
    p.step("gpsimd", lambda E: E.collective_compute(
        "AllGather", ALU.bypass, replica_groups=[list(range(8))],
        ins=[agin.ap()], outs=[agout.ap()]), inc=s_cc1)
    p.step("gpsimd", lambda E: E.collective_compute(
        "AllGather", ALU.bypass, replica_groups=[list(range(8))],
        ins=[ggin.ap()], outs=[ggout.ap()]), inc=s_cc2)
    C.join()
    p.wait("gpsimd", [(s_cc1, 1), (s_cc2, 1)])
    p.wait("tensor", [(s_cc1, 1), (s_cc2, 1)])
    p.wait("sync", [(s_cc1, 1), (s_cc2, 1)])
    p.wait("vector", [(s_cc1, 1), (s_cc2, 1)])
    p.wait("scalar", [(s_cc1, 1), (s_cc2, 1)])
    ao_flat = agout.ap().rearrange("r (x b) -> r x b", b=S)
    stq_flat = stq.ap().rearrange("r (x b) -> r x b", b=256)
    gg_flat = ggout.ap().rearrange("r (x b) -> r x b", b=S)
    stg_flat = stg2.ap().rearrange("r (x b) -> r x b", b=256)

    def stage(E):
        off = E.snap(E.partition_id() * 256, min_val=0, max_val=1792)
        for r in range(8):
            E.dma_start(out=stq_flat[r, :, :],
                        in_=ao_flat[r, :, bass.ds(off, 256)]
                        ).then_inc(C.s_w.h, 16)
        E.dma_start(out=stg_flat[:, :, :],
                    in_=gg_flat[:, :, bass.ds(off, 256)]
                    ).then_inc(C.s_w.h, 16)
        return E.nop()
    p.step("gpsimd", stage)
    C.s_w.n += 9 * 16
    stage_done = C.s_w.n
    p.wait("gpsimd", [(C.s_w, stage_done)])
    stq_re = stq.ap().rearrange("r (g f tl b) -> r g f tl b", g=3, f=D,
                                tl=2, b=256)
    stg_re = stg2.ap().rearrange("r (h tl b) -> r h tl b", h=16, b=256)

    BH = 32
    ao_re = agout.ap().rearrange("r (g f tl b) -> r g f tl b", g=3, f=D,
                                 tl=2, b=S)
    gg_re = ggout.ap().rearrange("r (h tl b) -> r h tl b", h=16, b=S)
    c2_re = c2in.ap().rearrange("o (t f b) -> o t f b", t=16, b=256)
    with C.scope() as esR:
        k_aug = C.sbs(esR, "k_aug", [128, 16 * 16 * BH], BF16)
        q_aug = C.sbs(esR, "q_aug", [128, 16 * 16 * BH], BF16)
        v_all = C.sbs(esR, "v_all", [16, 16 * 64 * BH], BF16)
        ones16 = C.sbs(esR, "ones16", [16, 1], BF16)
        ctx_a = C.sbs(esR, "ctx_a", [16, BH * 16 * 64], BF16)
        rec = C.sbs(esR, "rec", [16, 512], F32)
        p_sb = [C.sbs(esR, "p_sb", [16, 512], BF16) for _ in range(2)]
        p.step("vector", lambda E: E.memset(ones16[:, :], 1.0))
        p.step("vector", lambda E: E.memset(q_aug[64:80, :], 0.0))
        p.step("vector", lambda E: E.memset(k_aug[64:80, :], 0.0),
               inc=C.s_ve)
        z0 = C.s_ve.n
        wb4 = C.bank_acquire(4)
        for sj in range(16):
            p.step("tensor", lambda E, sj=sj: E.matmul(
                C.B[4][0:16, sj * 16:(sj + 1) * 16],
                oh_s[:, sj * 16:(sj + 1) * 16], rel_s[:, :],
                start=True, stop=True),
                waits=wb4 if sj == 0 else (),
                inc=C.s_mm if sj == 15 else None)
        pbd = C.s_mm.n
        p.step("vector", lambda E: E.tensor_copy(
            bass.AP(k_aug, 64 * k_aug.shape[1] if False else 0,
                    [[0, 0]]) if False else
            k_aug[64:80, :].rearrange("p (h s b) -> p h s b", h=16, b=BH),
            bass.AP(C.B[4], 0, [[512, 16], [1, 16], [16, 16], [0, BH]])),
            waits=[(C.s_mm, pbd), (C.s_ve, z0)], inc=C.s_ve)
        C.note_bank(4, C.s_ve)
        posb_done = C.s_ve.n
        p.wait("gpsimd", [(C.s_ve, posb_done)])
        av_done_by_bank = {}
        for rnd in range(256 // BH):
            b0 = rnd * BH
            lw = [(C.s_ve, C.s_ve.n), (C.s_mm, C.s_mm.n)]
            for src in range(8):
                for tl in range(2):
                    sg = src * 2 + tl
                    p.step("gpsimd",
                           lambda E, src=src, tl=tl, sg=sg, b0=b0:
                           E.dma_start(
                               out=k_aug[0:64, :].rearrange(
                                   "p (h s b) -> p h s b", h=16, b=BH)
                               [:, :, sg, :],
                               in_=stq_re[src, 1, :, tl, b0:b0 + BH]
                               .rearrange("(h d) b -> d h b", d=64)),
                           waits=lw if sg == 0 else (),
                           inc=C.s_w, inc_k=16)
                    p.step("gpsimd",
                           lambda E, src=src, tl=tl, sg=sg, b0=b0:
                           E.dma_start(
                               out=q_aug[0:64, :].rearrange(
                                   "p (h t b) -> p h t b", h=16, b=BH)
                               [:, :, sg, :],
                               in_=stq_re[src, 0, :, tl, b0:b0 + BH]
                               .rearrange("(h d) b -> d h b", d=64)),
                           inc=C.s_w, inc_k=16)
                    p.step("gpsimd",
                           lambda E, src=src, tl=tl, sg=sg, b0=b0:
                           E.dma_start(
                               out=v_all[sg:sg + 1, :].rearrange(
                                   "p (hd b) -> p hd b", b=BH),
                               in_=stq_re[src, 2, :, tl, b0:b0 + BH]),
                           inc=C.s_w, inc_k=16)
            p.step("vector", lambda E: E.nop(), waits=lw, inc=C.s_ve)
            for j in range(16):
                src, tl = j // 2, j % 2
                p.step("gpsimd", lambda E, j=j, src=src, tl=tl, b0=b0:
                       E.dma_start(
                           out=q_aug[64 + j:65 + j, :].rearrange(
                               "p (h t b) -> p h t b", h=16, b=BH)
                           [:, :, j, :],
                           in_=stg_re[src, :, tl, b0:b0 + BH]),
                       inc=C.s_w, inc_k=16)
            ldn = C.s_w.n
            vdn = C.s_ve.n
            if rnd == 0:
                dbg_dump(C, "k_aug", k_aug, [(C.s_w, ldn)])
                dbg_dump(C, "q_aug", q_aug, [(C.s_w, ldn)])
                dbg_dump(C, "v_all", v_all, [(C.s_w, ldn)])
            p.wait("tensor", [(C.s_w, ldn), (C.s_ve, vdn)])
            sums_w = C.bank_acquire(4)
            first_sum = True
            for g in range(16):
                qb = C.B[g % 2]
                qw = C.bank_acquire(g % 2)
                for i32 in range(32):
                    idx = g * 32 + i32
                    bb, h = idx // 16, idx % 16
                    p.step("tensor", lambda E, bb=bb, h=h, i32=i32, qb=qb:
                           E.matmul(
                               qb[0:16, i32 * 16:(i32 + 1) * 16],
                               k_aug[0:80, :].rearrange(
                                   "p (h s b) -> p h s b", h=16, b=BH)
                               [:, h, :, bb],
                               q_aug[0:80, :].rearrange(
                                   "p (h t b) -> p h t b", h=16, b=BH)
                               [:, h, :, bb],
                               start=True, stop=True),
                           waits=qw if i32 == 0 else (),
                           inc=C.s_mm if i32 == 31 else None)
                qkd = C.s_mm.n
                slot = p_sb[g % 2]
                ew = [(C.s_mm, qkd)]
                if g >= 2:
                    ew.append((C.s_mm, av_done_by_bank[g % 2]))
                p.step("scalar", lambda E, qb=qb, slot=slot: E.activation(
                    slot[:, :], qb[:, :] if False else qb[0:16, :],
                    AF.Exp), waits=ew, inc=C.s_sc)
                C.note_bank(g % 2, C.s_sc)
                expd = C.s_sc.n
                p.wait("tensor", [(C.s_sc, expd)])
                for i32 in range(32):
                    idx = g * 32 + i32
                    bb, h = idx // 16, idx % 16
                    avb = C.B[2 + (i32 // 8) % 2]
                    avw = C.bank_acquire(2 + (i32 // 8) % 2)
                    p.step("tensor", lambda E, bb=bb, h=h, i32=i32,
                           slot=slot, avb=avb: E.matmul(
                               avb[0:16, (i32 % 8) * 64:
                                   (i32 % 8) * 64 + 64],
                               slot[0:16, i32 * 16:(i32 + 1) * 16],
                               v_all[0:16, :].rearrange(
                                   "p (h d b) -> p h d b", h=16, d=64)
                               [:, h, 0:64, bb],
                               start=True, stop=True),
                           waits=avw if i32 % 8 == 0 else ())
                    p.step("tensor", lambda E, bb=bb, h=h, i32=i32,
                           slot=slot, idx=idx: E.matmul(
                               C.B[4][0:16, idx:idx + 1],
                               slot[0:16, i32 * 16:(i32 + 1) * 16],
                               ones16[:, 0:1],
                               start=True, stop=True),
                           waits=sums_w if first_sum else (),
                           inc=C.s_mm if i32 % 8 == 7 else None)
                    first_sum = False
                    if i32 % 8 == 7:
                        avd = C.s_mm.n
                        bsel = 2 + (i32 // 8) % 2
                        bb8 = (g * 32 + i32 - 7)
                        p.step("vector", lambda E, avb=avb, bb8=bb8:
                               E.tensor_copy(
                                   ctx_a[0:16, bb8 * 64:(bb8 + 8) * 64],
                                   avb[0:16, 0:512]),
                               waits=[(C.s_mm, avd)], inc=C.s_ve)
                        C.note_bank(bsel, C.s_ve)
                av_done_by_bank[g % 2] = C.s_mm.n
            alld = [(C.s_mm, C.s_mm.n), (C.s_ve, C.s_ve.n)]
            p.wait("vector", alld)
            p.step("vector", lambda E: E.tensor_copy(
                rec[0:16, :], C.B[4][0:16, :]), inc=C.s_ve)
            C.note_bank(4, C.s_ve)
            p.wait("vector", [(C.s_ve, C.s_ve.n)])
            p.step("vector", lambda E: E.reciprocal(
                rec[0:16, :], rec[0:16, :]), inc=C.s_ve)
            p.wait("vector", [(C.s_ve, C.s_ve.n)])
            p.step("vector", lambda E: E.tensor_tensor(
                ctx_a[0:16, :].rearrange(
                    "p (x d) -> p x d", d=64),
                ctx_a[0:16, :].rearrange(
                    "p (x d) -> p x d", d=64),
                bass.AP(rec, 0, [[512, 16], [1, 512], [0, 64]]),
                ALU.mult), inc=C.s_ve)
            nrm = C.s_ve.n
            if rnd == 0:
                dbg_dump(C, "ctx_a", ctx_a, [(C.s_ve, nrm)])
            p.step("sync", lambda E, rnd=rnd: E.dma_start(
                out=c2_re[0, :, rnd * BH:(rnd + 1) * BH, :]
                if False else
                c2in.ap().rearrange("o (t b f) -> o t b f", t=16, b=256)
                [0, :, rnd * BH:(rnd + 1) * BH, :],
                in_=ctx_a[0:16, :].rearrange(
                    "p (b h d) -> p b (h d)", b=BH, h=16)),
                waits=[(C.s_ve, nrm)], inc=C.s_st, inc_k=16)
            C.join()
    p.wait("gpsimd", [(C.s_st, C.s_st.n)])
    p.step("gpsimd", lambda E: E.collective_compute(
        "AllGather", ALU.bypass, replica_groups=[list(range(8))],
        ins=[c2in.ap()], outs=[c2out.ap()]), inc=s_cc3)
    C.join()
    for e in Prog.ENGINES:
        p.wait(e, [(s_cc3, 1)])
    c2o_dyn = c2out.ap().rearrange("r (t bf) -> r t bf", t=16)
    stc_flat = stc.ap().rearrange("r (t bf) -> r t bf", t=2)

    def stagec(E):
        offt = E.snap(E.partition_id() * 2, min_val=0, max_val=14)
        E.dma_start(out=stc_flat[:, :, :],
                    in_=c2o_dyn[:, bass.ds(offt, 2), :]
                    ).then_inc(C.s_w.h, 16)
        return E.nop()
    p.step("gpsimd", stagec)
    C.s_w.n += 16
    stc_done = C.s_w.n
    p.wait("gpsimd", [(C.s_w, stc_done)])
    stc_re = stc.ap().rearrange("r (t b f) -> r t b f", t=2, b=256)

    with C.scope() as esH:
        ctxN = C.sbs(esH, "ctxN", [128, D], BF16)
        ctxT = C.sbs(esH, "ctxT", [128, DT * T], BF16)
        y = C.sbs(esH, "y", [128, DT * TC], F32)
        yb = L.xcb
        xc = C.sbs(esH, "xc", [128, DT * TC], F32)
        wwH = C.sbs(esH, "wwH", [128, DT * DT * 128], BF16)
        for r8 in range(8):
            for tl in range(2):
                for bc in range(2):
                    p.step("gpsimd",
                           lambda E, r8=r8, tl=tl, bc=bc: E.dma_start(
                               out=ctxN[:, :],
                               in_=stc_re[r8, tl,
                                          bc * 128:(bc + 1) * 128, :]),
                           waits=[(C.s_ve, C.s_ve.n)],
                           inc=C.s_x, inc_k=16)
                    cl = C.s_x.n
                    tok0 = tl * S + r8 * 256 + bc * 128
                    if r8 == 0 and tl == 0 and bc == 0:
                        dbg_dump(C, "ctxN0", ctxN, [(C.s_x, cl)])
                    emit_transpose_TN(C, ctxN, 0, 128, ctxT, tok0, T,
                                      DT, 1, fence=[(C.s_x, cl)])
        p.wait("tensor", [(C.s_ve, C.s_ve.n)])
        dbg_dump(C, "ctxT", ctxT, [(C.s_ve, C.s_ve.n)])
        C.join()
        state = {"war": []}

        def ev_o(E, ps, nti, t0):
            return E.activation(
                y[:, nti * TC: nti * TC + TC], ps[:, :], AF.Identity,
                bias=biases["bo"][:, nti:nti + 1])

        def hook(tc, t0, state=state):
            yr = [(C.s_sc, C.s_sc.n)]
            if tc == 0:
                dbg_dump(C, "y0", y, yr)
            stc, od = emit_post_ln_res(C, L, y, yb, x_in, x_out, t0=t0,
                                       y_ready=yr + state["war"], xc=xc)
            state["war"] = [(C.s_ve, od)]
            p.wait("scalar", [(C.s_ve, od)])
            p.wait("sync", [(C.s_ve, od)])
            p.wait("gpsimd", [(C.s_ve, od)])
            C.join()
        linear_layer(C, x_sb=ctxT, KT=DT, W=wod, n_tiles=DT, evac=ev_o,
                     x_Ttot=T, slab_nt=DT, wbufs=[wwH], tc_outer=True,
                     chunk_hook=hook)
        dbg_dump(C, "ctxT2", ctxT, [(C.s_ve, C.s_ve.n)])
        dbg_dump(C, "wwH", wwH, [(C.s_ve, C.s_ve.n)])
    C.join()


def prep_l1_attn(inp, xT_list):
    import math
    NB, MD = 32, 128

    def bucket(rp):
        nb = NB // 2
        b = (rp > 0).astype(np.int32) * nb
        rpa = np.abs(rp)
        mx = nb // 2
        with np.errstate(divide="ignore"):
            large = mx + (np.log(np.maximum(rpa, 1) / mx)
                          / math.log(MD / mx) * (nb - mx)).astype(np.int32)
        large = np.minimum(large, nb - 1)
        return b + np.where(rpa < mx, rpa, large)
    rp = np.arange(16)[None, :] - np.arange(16)[:, None]
    bk = bucket(rp)
    oh = np.zeros((32, 256), np.float32)
    for t_ in range(16):
        for s_ in range(16):
            oh[bk[t_, s_], s_ * 16 + t_] = 1.0
    ind = np.zeros((1, 256), np.float32)
    ind[0, 0:16] = 1.0
    ind[0, 128 + 64:128 + 80] = 1.0
    ga = np.zeros((128, 1), np.float32)
    gav = np.asarray(inp["L1_attn_grep_a"], np.float32).reshape(16)
    ga[64:80, 0] = gav
    shared = {
        "wq": np.ascontiguousarray(np.asarray(inp["L1_attn_q_w"],
                                              np.float32).T),
        "bq": to_cols(inp["L1_attn_q_b"]),
        "wk": np.ascontiguousarray(np.asarray(inp["L1_attn_k_w"],
                                              np.float32).T),
        "bk": to_cols(inp["L1_attn_k_b"]),
        "wv": np.ascontiguousarray(np.asarray(inp["L1_attn_v_w"],
                                              np.float32).T),
        "bv": to_cols(inp["L1_attn_v_b"]),
        "wo": np.ascontiguousarray(np.asarray(inp["L1_attn_out_w"],
                                              np.float32).T),
        "bo": to_cols(inp["L1_attn_out_b"]),
        "rel": np.asarray(inp["L1_attn_rel_emb"], np.float32),
        "gw": np.asarray(inp["L1_attn_grep_w"], np.float32).T.copy(),
        "gb": np.asarray(inp["L1_attn_grep_b"], np.float32).reshape(1, 8),
        "ga": ga,
        "c_oh": oh,
        "c_ind": ind,
        "c_identity": np.eye(128, dtype=np.float32),
    }
    return [dict(shared, x_in=xT_list[c]) for c in range(NCORES)]



def run_launch(build_fn, in_maps):
    C = Ctx()
    build_fn(C)
    C.p.emit()
    res = run_bass_kernel_spmd(C.nc, in_maps, core_ids=list(range(NCORES)))
    return res.results


SUBLAYERS = [
    ("l0a", build_l0_attn, prep_l0_attn),
    ("l0f", build_tern_ffn, prep_tern_ffn("L0_ffn")),
    ("l2a", build_l2_attn, prep_l2_attn),
    ("l2f", build_tern_ffn, prep_tern_ffn("L2_ffn")),
    ("l3a", build_l3_attn, prep_l3_attn),
    ("l1f", build_l1_ffn, prep_l1_ffn),
    ("l3f", build_l3_ffn, prep_l3_ffn),
    ("l1a", build_l1_attn, prep_l1_attn),
]


EXEC_ORDER = ["l0a", "l0f", "l1a", "l1f", "l2a", "l2f", "l3a", "l3f"]


def kernel(**inputs):
    x0 = np.asarray(inputs["L0_attn_hidden_states"], np.float32)
    xT = [to_T(x0[2 * c:2 * c + 2].reshape(T, D)) for c in range(NCORES)]
    kmap = {name: (b, p) for name, b, p in SUBLAYERS}
    for name in EXEC_ORDER:
        build_fn, prep_fn = kmap[name]
        in_maps = prep_fn(inputs, xT)
        results = run_launch(build_fn, in_maps)
        xT = [r["x_out"] for r in results]
    out = np.empty((16, S, D), np.float32)
    for c in range(NCORES):
        out[2 * c:2 * c + 2] = from_T(xT[c], T, D).reshape(2, S, D)
    return out








